# version 28
# baseline (speedup 1.0000x reference)
import numpy as np
import ml_dtypes
import concourse.bass as bass
import concourse.mybir as mybir
from concourse.bass_utils import run_bass_kernel_spmd

F32 = mybir.dt.float32
BF16 = mybir.dt.bfloat16
Act = mybir.ActivationFunctionType
Alu = mybir.AluOpType
AX = mybir.AxisListType.X

D = 1024
NDC = 8
E = 8
C_ONE, C_ZERO, C_EPS, C_INV1024, C_NEG05, C_1_32, C_NEG8 = range(7)
CONST_VALS = [1.0, 0.0, 1e-5, 1.0 / 1024.0, -0.5, 1.0 / 32.0, -8.0]


def build_nc(S=4096):
    NT = S // 128
    NT2 = NT // 2
    NW = S // 512
    NQS = S // 256
    NKT = S // 128
    NK = 32 * NQS
    NST = S // 512
    CAPT = 2 * (S // 128) // E
    CAP = CAPT * 128
    CAPTOT = E * CAP
    NCT = E * CAPT

    nc = bass.Bass("TRN2", target_bir_lowering=False)

    x_h = nc.dram_tensor("x", [S, D], F32, kind="ExternalInput").ap()
    wqT_h = nc.dram_tensor("wqT", [D, D], F32, kind="ExternalInput").ap()
    wkT_h = nc.dram_tensor("wkT", [D, D], F32, kind="ExternalInput").ap()
    wvT_h = nc.dram_tensor("wvT", [D, D], F32, kind="ExternalInput").ap()
    ewT_h = nc.dram_tensor("ewT", [E, D, D], F32, kind="ExternalInput").ap()
    gw_h = nc.dram_tensor("gwT", [128, NDC * E], F32, kind="ExternalInput").ap()
    bqc_h = nc.dram_tensor("bqc", [128, 8], F32, kind="ExternalInput").ap()
    bkc_h = nc.dram_tensor("bkc", [128, 8], F32, kind="ExternalInput").ap()
    bvb_h = nc.dram_tensor("bvb", [128, D], F32, kind="ExternalInput").ap()
    cst_h = nc.dram_tensor("cst", [128, 8], F32, kind="ExternalInput").ap()
    idm_h = nc.dram_tensor("idm", [128, 128], F32, kind="ExternalInput").ap()
    idmb_h = nc.dram_tensor("idmb", [128, 128], BF16, kind="ExternalInput").ap()
    ones_h = nc.dram_tensor("onesb", [128, 1], BF16, kind="ExternalInput").ap()
    iotac_h = nc.dram_tensor("iotac", [128, 1], F32, kind="ExternalInput").ap()
    iotat_h = nc.dram_tensor("iotat", [128, NT], F32, kind="ExternalInput").ap()
    dumpf_h = nc.dram_tensor("dumpf", [CAPTOT // 128, 128], F32, kind="ExternalInput").ap()
    zrow_h = nc.dram_tensor("zrow", [1, D], BF16, kind="ExternalInput").ap()
    zrowf_h = nc.dram_tensor("zrowf", [1, D], F32, kind="ExternalInput").ap()
    out_h = nc.dram_tensor("out", [S, D], F32, kind="ExternalOutput").ap()

    s_xnT = nc.dram_tensor("s_xnT", [NDC, 128, S], BF16, kind="Internal").ap()
    s_QT = nc.dram_tensor("s_QT", [NDC, 128, S], BF16, kind="Internal").ap()
    s_KT = nc.dram_tensor("s_KT", [NDC, 128, S], BF16, kind="Internal").ap()
    s_V = nc.dram_tensor("s_V", [128, NT * D], BF16, kind="Internal").ap()
    s_xpn = nc.dram_tensor("s_xpn", [S, D], F32, kind="Internal").ap()
    s_pB = nc.dram_tensor("s_pB", [S + 1, D], F32, kind="Internal").ap()
    s_xnb = nc.dram_tensor("s_xnb", [S + 1, D], BF16, kind="Internal").ap()
    s_perm = nc.dram_tensor("s_perm", [CAPTOT // 128, 128], F32, kind="Internal").ap()
    s_m1 = nc.dram_tensor("s_m1", [NT, 128], F32, kind="Internal").ap()
    s_m2 = nc.dram_tensor("s_m2", [1, S], F32, kind="Internal").ap()

    import contextlib
    with contextlib.ExitStack() as _es:
        consts = _es.enter_context(nc.sbuf_tensor("consts", [128, 8], F32))
        ident = _es.enter_context(nc.sbuf_tensor("ident", [128, 128], F32))
        ones_s = _es.enter_context(nc.sbuf_tensor("ones_s", [128, 1], BF16))
        idx_res = _es.enter_context(nc.sbuf_tensor("idx_res", [128, NT], F32))
        identb = _es.enter_context(nc.sbuf_tensor("identb", [128, 128], BF16))
        iotac_s = _es.enter_context(nc.sbuf_tensor("iotac_s", [128, 1], F32))
        iotat_s = _es.enter_context(nc.sbuf_tensor("iotat_s", [128, NT], F32))
        a_pre = _es.enter_context(nc.semaphore("a_pre"))
        a_w = _es.enter_context(nc.semaphore("a_w"))
        a_x = _es.enter_context(nc.semaphore("a_x"))
        a_vs = _es.enter_context(nc.semaphore("a_vs"))
        a_sx = _es.enter_context(nc.semaphore("a_sx"))
        a_xn = _es.enter_context(nc.semaphore("a_xn"))
        a_xpn = _es.enter_context(nc.semaphore("a_xpn"))
        a_tp = _es.enter_context(nc.semaphore("a_tp"))
        a_evs = _es.enter_context(nc.semaphore("a_evs"))
        a_evv = _es.enter_context(nc.semaphore("a_evv"))
        a_mmq = _es.enter_context(nc.semaphore("a_mmq"))
        a_evq = _es.enter_context(nc.semaphore("a_evq"))
        a_mmk = _es.enter_context(nc.semaphore("a_mmk"))
        a_evk = _es.enter_context(nc.semaphore("a_evk"))
        a_mmv = _es.enter_context(nc.semaphore("a_mmv"))
        a_evV = _es.enter_context(nc.semaphore("a_evV"))
        a_mmg = _es.enter_context(nc.semaphore("a_mmg"))
        a_evg = _es.enter_context(nc.semaphore("a_evg"))
        a_gt = _es.enter_context(nc.semaphore("a_gt"))
        st_xpn = [_es.enter_context(nc.semaphore(f"st_xpn{p}")) for p in range(2)]
        st_xnt = [_es.enter_context(nc.semaphore(f"st_xnt{p}")) for p in range(2)]
        st_qt = _es.enter_context(nc.semaphore("st_qt"))
        st_kt = _es.enter_context(nc.semaphore("st_kt"))
        st_v = _es.enter_context(nc.semaphore("st_v"))
        st_xnb = _es.enter_context(nc.semaphore("st_xnb"))
        a_xnb = _es.enter_context(nc.semaphore("a_xnb"))
        a_xp2 = [_es.enter_context(nc.semaphore(f"a_xp{p}")) for p in range(2)]
        b_ldk = _es.enter_context(nc.semaphore("b_ldk"))
        b_ldk2 = _es.enter_context(nc.semaphore("b_ldk2"))
        b_ldv = _es.enter_context(nc.semaphore("b_ldv"))
        b_qt2 = [_es.enter_context(nc.semaphore(f"b_qt{p}")) for p in range(2)]
        b_xp2 = [_es.enter_context(nc.semaphore(f"b_xp{p}")) for p in range(2)]
        b_x2 = [_es.enter_context(nc.semaphore(f"b_x{p}")) for p in range(2)]
        b_vo2 = _es.enter_context(nc.semaphore("b_vo2"))
        b_sht = _es.enter_context(nc.semaphore("b_sht"))
        b_exp = _es.enter_context(nc.semaphore("b_exp"))
        b_pv = _es.enter_context(nc.semaphore("b_pv"))
        b_vo = _es.enter_context(nc.semaphore("b_vo"))
        b_st = _es.enter_context(nc.semaphore("b_st"))
        c_s1 = _es.enter_context(nc.semaphore("c_s1"))
        c_t1 = _es.enter_context(nc.semaphore("c_t1"))
        c_e1 = _es.enter_context(nc.semaphore("c_e1"))
        c_t2 = _es.enter_context(nc.semaphore("c_t2"))
        c_t3 = _es.enter_context(nc.semaphore("c_t3"))
        c_s2 = _es.enter_context(nc.semaphore("c_s2"))
        c_s3 = _es.enter_context(nc.semaphore("c_s3"))
        c_s4 = _es.enter_context(nc.semaphore("c_s4"))
        c_s5 = _es.enter_context(nc.semaphore("c_s5"))
        c_v1 = _es.enter_context(nc.semaphore("c_v1"))
        c_v2 = _es.enter_context(nc.semaphore("c_v2"))
        c_v3 = _es.enter_context(nc.semaphore("c_v3"))
        c_v4 = _es.enter_context(nc.semaphore("c_v4"))
        c_mr = _es.enter_context(nc.semaphore("c_mr"))
        c_rv = _es.enter_context(nc.semaphore("c_rv"))
        c_gp = _es.enter_context(nc.semaphore("c_gp"))
        c_gs = _es.enter_context(nc.semaphore("c_gs"))
        c_g2 = [_es.enter_context(nc.semaphore(f"c_g{p}")) for p in range(4)]
        c_wl = [_es.enter_context(nc.semaphore(f"c_wl{p}")) for p in range(2)]
        c_tp2 = _es.enter_context(nc.semaphore("c_tp2"))
        c_xe = _es.enter_context(nc.semaphore("c_xe"))
        c_my = _es.enter_context(nc.semaphore("c_my"))
        c_ye = _es.enter_context(nc.semaphore("c_ye"))
        c_pg = [_es.enter_context(nc.semaphore(f"c_pg{p}")) for p in range(4)]
        c_os = [_es.enter_context(nc.semaphore(f"c_os{p}")) for p in range(4)]
        c_oa = _es.enter_context(nc.semaphore("c_oa"))
        c_z = _es.enter_context(nc.semaphore("c_z"))
        for j, v in enumerate(CONST_VALS):
            nc.const_aps.register(F32, v, consts[:, j:j + 1])

        def cc(j):
            return consts[:, j:j + 1]

        with contextlib.ExitStack() as _ps:
            wq_b = _ps.enter_context(nc.sbuf_tensor("wq_b", [128, NDC * D], BF16))
            wk_b = _ps.enter_context(nc.sbuf_tensor("wk_b", [128, NDC * D], BF16))
            wv_b = _ps.enter_context(nc.sbuf_tensor("wv_b", [128, NDC * D], BF16))
            gw_s = _ps.enter_context(nc.sbuf_tensor("gw_s", [128, NDC * E], F32))
            bqc_s = _ps.enter_context(nc.sbuf_tensor("bqc_s", [128, 8], F32))
            bkc_s = _ps.enter_context(nc.sbuf_tensor("bkc_s", [128, 8], F32))
            bvb_s = _ps.enter_context(nc.sbuf_tensor("bvb_s", [128, D], F32))
            xwin = _ps.enter_context(nc.sbuf_tensor("xwin", [128, 2 * 4 * D], F32))
            xnwin = _ps.enter_context(nc.sbuf_tensor("xnwin", [128, 2 * 4 * D], F32))
            sqdump = _ps.enter_context(nc.sbuf_tensor("sqdump", [128, D], F32))
            stats = _ps.enter_context(nc.sbuf_tensor("stats", [128, 4 * 8], F32))
            xnb_sb = _ps.enter_context(nc.sbuf_tensor("xnb_sb", [128, 4 * D], BF16))
            xnT_b = _ps.enter_context(nc.sbuf_tensor("xnT_b", [128, 2 * NDC * 512], BF16))
            xnT_f = _ps.enter_context(nc.sbuf_tensor("xnT_f", [128, 2 * NDC * 512], F32))
            qt_sb = _ps.enter_context(nc.sbuf_tensor("qt_sb", [128, NDC * 512], BF16))
            kt_sb = _ps.enter_context(nc.sbuf_tensor("kt_sb", [128, NDC * 512], BF16))
            vt_sb = _ps.enter_context(nc.sbuf_tensor("vt_sb", [128, 4 * D], BF16))
            gT_sb = _ps.enter_context(nc.sbuf_tensor("gT_sb", [128, 512], F32))
            gtok = _ps.enter_context(nc.sbuf_tensor("gtok", [128, 8], F32))
            gmax = _ps.enter_context(nc.sbuf_tensor("gmax", [128, 8], F32))
            gidx = _ps.enter_context(nc.sbuf_tensor("gidx", [128, 8], mybir.dt.uint32))
            pT0 = _ps.enter_context(nc.psum_tensor("pT0", [128, 512], F32))
            pT1 = _ps.enter_context(nc.psum_tensor("pT1", [128, 512], F32))
            pQK0 = _ps.enter_context(nc.psum_tensor("pQK0", [128, 512], F32))
            pQK1 = _ps.enter_context(nc.psum_tensor("pQK1", [128, 512], F32))
            pV0 = _ps.enter_context(nc.psum_tensor("pV0", [128, 512], F32))
            pV1 = _ps.enter_context(nc.psum_tensor("pV1", [128, 512], F32))
            pV2 = _ps.enter_context(nc.psum_tensor("pV2", [128, 512], F32))
            pV3 = _ps.enter_context(nc.psum_tensor("pV3", [128, 512], F32))
            blk = _ps.enter_context(nc.Block("phA"))
            pT = [pT0, pT1]
            pQK = [pQK0, pQK1]
            pV = [pV0, pV1, pV2, pV3]
            xnTb_v = xnT_b[:].rearrange("p (a c t s) -> p a c t s", a=2, c=NDC, t=4)
            xnTf_v = xnT_f[:].rearrange("p (a c t s) -> p a c t s", a=2, c=NDC, t=4)
            pT0_v = pT0[:].rearrange("p (c s) -> p c s", c=4)
            pT1_v = pT1[:].rearrange("p (c s) -> p c s", c=4)

            @blk.sync
            def _(sy):
                sy.dma_start(consts[:], cst_h).then_inc(a_pre, 16)
                sy.dma_start(ident[:], idm_h).then_inc(a_pre, 16)
                sy.dma_start(ones_s[:], ones_h).then_inc(a_pre, 16)
                sy.dma_start(gw_s[:], gw_h).then_inc(a_pre, 16)
                sy.dma_start(bqc_s[:], bqc_h).then_inc(a_pre, 16)
                sy.dma_start(bkc_s[:], bkc_h).then_inc(a_pre, 16)
                sy.dma_start(bvb_s[:], bvb_h).then_inc(a_pre, 16)
                sy.dma_start(identb[:], idmb_h).then_inc(a_pre, 16)
                sy.dma_start(iotac_s[:], iotac_h).then_inc(a_pre, 16)
                sy.dma_start(iotat_s[:], iotat_h).then_inc(a_pre, 16)
                sy.dma_start(s_xnb[S:S + 1, :], zrow_h).then_inc(a_pre, 16)
                sy.dma_start(s_pB[S:S + 1, :], zrowf_h).then_inc(c_z, 16)
                for w in range(NW):
                    for t in range(4):
                        i = 4 * w + t
                        sy.wait_ge(a_xn, i + 1)
                        sy.dma_start(s_xpn[i * 128:(i + 1) * 128, :],
                                     xnwin[:, (w % 2) * 4 * D + t * D:
                                           (w % 2) * 4 * D + (t + 1) * D]
                                     ).then_inc(st_xpn[w % 2], 16)
                    for t in range(4):
                        i = 4 * w + t
                        sy.wait_ge(a_xnb, i + 1)
                        sy.dma_start(s_xnb[i * 128:(i + 1) * 128, :],
                                     xnb_sb[:, t * D:(t + 1) * D]
                                     ).then_inc(st_xnb, 16)
                    sy.wait_ge(a_evs, 4 * w + 4)
                    sy.wait_ge(a_evv, 4 * w + 4)
                    for dc in range(NDC):
                        sy.dma_start(
                            s_xnT[dc, :, w * 512:(w + 1) * 512],
                            xnT_b[:, (w % 2) * 4096 + dc * 512:
                                  (w % 2) * 4096 + (dc + 1) * 512]
                        ).then_inc(st_xnt[w % 2], 16)
                    for ec in range(NDC):
                        sy.wait_ge(a_evq, 8 * w + ec + 1)
                        sy.dma_start(s_QT[ec, :, w * 512:(w + 1) * 512],
                                     qt_sb[:, ec * 512:(ec + 1) * 512]
                                     ).then_inc(st_qt, 16)
                    for ec in range(NDC):
                        sy.wait_ge(a_evk, 8 * w + ec + 1)
                        sy.dma_start(s_KT[ec, :, w * 512:(w + 1) * 512],
                                     kt_sb[:, ec * 512:(ec + 1) * 512]
                                     ).then_inc(st_kt, 16)
                    sy.wait_ge(a_evV, 4 * w + 4)
                    sy.dma_start(s_V[:, w * 4 * D:(w + 1) * 4 * D],
                                 vt_sb[:]).then_inc(st_v, 16)

            @blk.gpsimd
            def _(g):
                wjobs = [(wq_b, wqT_h), (wk_b, wkT_h), (wv_b, wvT_h)]
                def load_x(w):
                    if w >= 2:
                        g.wait_ge(a_xn, 4 * (w - 1))
                    for t in range(4):
                        i = 4 * w + t
                        g.dma_start(xwin[:, (w % 2) * 4 * D + t * D:
                                         (w % 2) * 4 * D + (t + 1) * D],
                                    x_h[i * 128:(i + 1) * 128, :]).then_inc(a_xp2[w % 2], 16)
                def load_w(j):
                    wsb, wh = wjobs[j]
                    for dc in range(NDC):
                        g.dma_start(wsb[:, dc * D:(dc + 1) * D],
                                    wh[dc * 128:(dc + 1) * 128, :]).then_inc(a_w, 16)
                load_x(0)
                load_w(0)
                if NW > 1:
                    load_x(1)
                load_w(1)
                load_w(2)
                for w in range(2, NW):
                    load_x(w)

            @blk.vector
            def _(v):
                v.wait_ge(a_pre, 176)
                v.memset(gmax[:], 0)
                v.drain()

                def v_stat1(w, t):
                    par = w % 2
                    i = 4 * w + t
                    xt = xwin[:, par * 4 * D + t * D:par * 4 * D + (t + 1) * D]
                    xn = xnwin[:, par * 4 * D + t * D:par * 4 * D + (t + 1) * D]
                    ss = stats[:, t * 8:t * 8 + 8]
                    if t == 0:
                        v.wait_ge(a_xp2[w % 2], 64 * (w // 2 + 1))
                    v.tensor_reduce(ss[:, 0:1], xt, axis=AX, op=Alu.add)
                    v.drain()
                    v.tensor_scalar(ss[:, 1:2], ss[:, 0:1], -1.0 / 1024.0, None,
                                    op0=Alu.mult).then_inc(a_vs, 1)
                    v.wait_ge(a_sx, i + 1)
                    v.tensor_tensor(ss[:, 5:6], ss[:, 1:2], ss[:, 4:5],
                                    op=Alu.mult)
                    v.drain()
                    if i >= 8:
                        v.wait_ge(a_tp, 2 * (i - 8) + 2)
                        v.wait_ge(st_xpn[par], 64 * (w // 2))
                        v.wait_ge(a_xnb, i - 4)
                    v.tensor_scalar(xn, xt, ss[:, 4:5], ss[:, 5:6],
                                    op0=Alu.mult, op1=Alu.add).then_inc(a_xn, 1)

                def v_kev(w, ec):
                    v.wait_ge(a_mmk, 8 * w + ec + 1)
                    if ec == 0 and w >= 1:
                        v.wait_ge(st_kt, 128 * w)
                    v.tensor_scalar(kt_sb[:, ec * 512:(ec + 1) * 512],
                                    pQK[ec % 2][:, 0:512], bkc_s[:, ec:ec + 1],
                                    None, op0=Alu.add).then_inc(a_evk, 1)

                def v_pt1_1(w, t):
                    par = w % 2
                    i = 4 * w + t
                    v.wait_ge(a_tp, 2 * i + 2)
                    if w >= 2:
                        v.wait_ge(a_mmg, w - 1)
                        v.wait_ge(st_xnt[w % 2], 128 * (w // 2))
                    v.tensor_copy(xnTb_v[:, par, 4:8, t, :], pT1_v)
                    v.tensor_copy(xnTf_v[:, par, 4:8, t, :],
                                  pT1_v).then_inc(a_evv, 1)

                def v_pt1(w):
                    for t in range(4):
                        v_pt1_1(w, t)

                def v_post(w):
                    for t in range(4):
                        i = 4 * w + t
                        v.wait_ge(a_mmv, i + 1)
                        if w >= 1 and t == 0:
                            v.wait_ge(st_v, 16 * w)
                        for h in range(2):
                            tt = v.tensor_tensor(
                                vt_sb[:, t * D + h * 512:t * D + (h + 1) * 512],
                                pV[2 * (t % 2) + h][:, 0:512],
                                bvb_s[:, h * 512:(h + 1) * 512], op=Alu.add)
                            if h == 1:
                                tt.then_inc(a_evV, 1)
                    v.wait_ge(a_mmg, w + 1)
                    v.tensor_copy(gT_sb[0:8, :], pQK[0][0:8, :]).then_inc(a_evg, 1)
                    v.wait_ge(a_gt, w + 1)
                    for t in range(4):
                        i = 4 * w + t
                        v.drain()
                        v.tensor_copy(gtok[:], pQK[1][:, t * 8:t * 8 + 8])
                        v.drain()
                        v.tensor_reduce(gmax[:, 0:1], gtok[:], axis=AX, op=Alu.max)
                        v.drain()
                        v.max_index(gidx[:], gmax[:], gtok[:])
                        v.drain()
                        mi = v.tensor_copy(idx_res[:, i:i + 1], gidx[:, 0:1])
                        if t == 3:
                            mi.then_inc(a_evg, 1)

                for t in range(4):
                    v_stat1(0, t)
                v_pt1(0)
                for w in range(NW):
                    if w + 1 < NW:
                        v_stat1(w + 1, 0)
                        v_stat1(w + 1, 1)
                        v_kev(w, 0); v_kev(w, 1)
                        v_kev(w, 2); v_kev(w, 3)
                        v_stat1(w + 1, 2)
                        v_kev(w, 4); v_kev(w, 5)
                        v_kev(w, 6); v_kev(w, 7)
                        v_stat1(w + 1, 3)
                    else:
                        for ec in range(NDC):
                            v_kev(w, ec)
                    v_post(w)
                    if w + 1 < NW:
                        v_pt1(w + 1)

            @blk.scalar
            def _(s):
                s.wait_ge(a_pre, 176)

                def s_ln(w, t):
                    par = w % 2
                    i = 4 * w + t
                    xt = xwin[:, par * 4 * D + t * D:par * 4 * D + (t + 1) * D]
                    ss = stats[:, t * 8:t * 8 + 8]
                    s.wait_ge(a_vs, i + 1)
                    if i >= 4:
                        s.wait_ge(a_xn, i - 3)
                    s.activation(sqdump[:], xt, Act.Square, bias=ss[:, 1:2],
                                 scale=cc(C_ONE), accum_out=ss[:, 2:3])
                    s.drain()
                    s.activation(ss[:, 3:4], ss[:, 2:3], Act.Ln,
                                 bias=cc(C_EPS), scale=cc(C_INV1024))
                    s.drain()
                    s.activation(ss[:, 4:5], ss[:, 3:4], Act.Exp,
                                 bias=cc(C_ZERO), scale=cc(C_NEG05)).then_inc(a_sx, 1)

                def s_qev(w, ec):
                    s.wait_ge(a_mmq, 8 * w + ec + 1)
                    if ec == 0 and w >= 1:
                        s.wait_ge(st_qt, 128 * w)
                    s.activation(qt_sb[:, ec * 512:(ec + 1) * 512],
                                 pQK[ec % 2][:, 0:512], Act.Identity,
                                 bias=bqc_s[:, ec:ec + 1],
                                 scale=cc(C_ONE)).then_inc(a_evq, 1)

                def s_xnb(w, t):
                    par = w % 2
                    i = 4 * w + t
                    s.wait_ge(a_xn, i + 1)
                    if w >= 1 and t == 0:
                        s.wait_ge(st_xnb, 64 * w)
                    s.activation(xnb_sb[:, t * D:(t + 1) * D],
                                 xnwin[:, par * 4 * D + t * D:
                                       par * 4 * D + (t + 1) * D],
                                 Act.Identity, bias=cc(C_ZERO),
                                 scale=cc(C_ONE)).then_inc(a_xnb, 1)

                def s_xnt(w, t):
                    par = w % 2
                    i = 4 * w + t
                    s.wait_ge(a_tp, 2 * i + 1)
                    if w >= 2:
                        s.wait_ge(a_mmg, w - 1)
                        s.wait_ge(st_xnt[w % 2], 128 * (w // 2))
                    s.activation(xnTb_v[:, par, 0:4, t, :], pT0_v, Act.Identity,
                                 bias=cc(C_ZERO), scale=cc(C_ONE))
                    s.activation(xnTf_v[:, par, 0:4, t, :], pT0_v, Act.Identity,
                                 bias=cc(C_ZERO),
                                 scale=cc(C_ONE)).then_inc(a_evs, 1)

                for t in range(4):
                    s_ln(0, t)
                for t in range(4):
                    s_xnb(0, t)
                for t in range(4):
                    s_xnt(0, t)
                for w in range(NW):
                    if w + 1 < NW:
                        s_qev(w, 0); s_qev(w, 1); s_qev(w, 2); s_qev(w, 3)
                        s_ln(w + 1, 0)
                        s_qev(w, 4); s_qev(w, 5); s_qev(w, 6); s_qev(w, 7)
                        s_ln(w + 1, 1)
                        s_ln(w + 1, 2)
                        s_ln(w + 1, 3)
                        for t in range(4):
                            s_xnb(w + 1, t)
                        for t in range(4):
                            s_xnt(w + 1, t)
                    else:
                        for ec in range(NDC):
                            s_qev(w, ec)

            @blk.tensor
            def _(t_):
                t_.wait_ge(a_pre, 176)
                t_.wait_ge(a_w, 16 * 3 * NDC)

                def tp(w, t):
                    par = w % 2
                    i = 4 * w + t
                    t_.wait_ge(a_xn, i + 1)
                    if i >= 1:
                        t_.wait_ge(a_evs, i)
                        t_.wait_ge(a_evv, i)
                    xn = xnwin[:, par * 4 * D + t * D:par * 4 * D + (t + 1) * D]
                    for dc in range(8):
                        tr = t_.transpose(pT[dc // 4][:, (dc % 4) * 128:
                                                      (dc % 4 + 1) * 128],
                                          xn[:, dc * 128:(dc + 1) * 128],
                                          ident[:])
                        if dc % 4 == 3:
                            tr.then_inc(a_tp, 1)

                for w in range(NW):
                    par = w % 2
                    for t in range(4):
                        tp(w, t)
                    for ec in range(NDC):
                        if ec < 2:
                            if w >= 1:
                                t_.wait_ge(a_evg, 2 * w)
                            if ec == 0:
                                t_.wait_ge(a_evs, 4 * w + 4)
                                t_.wait_ge(a_evv, 4 * w + 4)
                        else:
                            t_.wait_ge(a_evq, 8 * w + ec - 1)
                        for dc in range(NDC):
                            mm = t_.matmul(pQK[ec % 2][:, 0:512],
                                           wq_b[:, dc * D + ec * 128:
                                                dc * D + (ec + 1) * 128],
                                           xnT_b[:, par * 4096 + dc * 512:
                                                 par * 4096 + (dc + 1) * 512],
                                           start=(dc == 0), stop=(dc == NDC - 1))
                        mm.then_inc(a_mmq, 1)
                    for ec in range(NDC):
                        if ec < 2:
                            t_.wait_ge(a_evq, 8 * w + ec + 7)
                        else:
                            t_.wait_ge(a_evk, 8 * w + ec - 1)
                        for dc in range(NDC):
                            mm = t_.matmul(pQK[ec % 2][:, 0:512],
                                           wk_b[:, dc * D + ec * 128:
                                                dc * D + (ec + 1) * 128],
                                           xnT_b[:, par * 4096 + dc * 512:
                                                 par * 4096 + (dc + 1) * 512],
                                           start=(dc == 0), stop=(dc == NDC - 1))
                        mm.then_inc(a_mmk, 1)
                    for t in range(4):
                        i = 4 * w + t
                        if i >= 2:
                            t_.wait_ge(a_evV, i - 1)
                        for dc in range(NDC):
                            for h in range(2):
                                mm = t_.matmul(pV[2 * (t % 2) + h][:, 0:512],
                                               xnT_b[:, par * 4096 + dc * 512 + t * 128:
                                                     par * 4096 + dc * 512 + (t + 1) * 128],
                                               wv_b[:, dc * D + h * 512:
                                                    dc * D + (h + 1) * 512],
                                               start=(dc == 0), stop=(dc == NDC - 1))
                        mm.then_inc(a_mmv, 1)
                    t_.wait_ge(a_evk, 8 * w + 7)
                    for dc in range(NDC):
                        mm = t_.matmul(pQK[0][0:8, 0:512],
                                       gw_s[:, dc * 8:(dc + 1) * 8],
                                       xnT_f[:, par * 4096 + dc * 512:
                                             par * 4096 + (dc + 1) * 512],
                                       start=(dc == 0), stop=(dc == NDC - 1))
                    mm.then_inc(a_mmg, 1)
                    t_.wait_ge(a_evk, 8 * w + 8)
                    t_.wait_ge(a_evg, 2 * w + 1)
                    for t in range(4):
                        tr = t_.transpose(pQK[1][:, t * 8:(t + 1) * 8],
                                          gT_sb[0:8, t * 128:(t + 1) * 128],
                                          ident[0:8, 0:8])
                        if t == 3:
                            tr.then_inc(a_gt, 1)

        with contextlib.ExitStack() as _ps:
            KT_res = _ps.enter_context(nc.sbuf_tensor("KT_res", [128, NDC * S], BF16))
            V_res = _ps.enter_context(nc.sbuf_tensor("V_res", [128, NT * D], BF16))
            qt_buf = _ps.enter_context(nc.sbuf_tensor("qt_buf", [128, 2 * NDC * 256], BF16))
            pt_buf = _ps.enter_context(nc.sbuf_tensor("pt_buf", [128, 4 * 256], BF16))
            xpn_buf = _ps.enter_context(nc.sbuf_tensor("xpn_buf", [128, 2 * 2048], F32))
            xw_buf = _ps.enter_context(nc.sbuf_tensor("xw_buf", [128, 2 * 2048], F32))
            linv = _ps.enter_context(nc.sbuf_tensor("linv", [128, 2], F32))
            pS0 = _ps.enter_context(nc.psum_tensor("pS0", [128, 512], F32))
            pS1 = _ps.enter_context(nc.psum_tensor("pS1", [128, 512], F32))
            pO0 = _ps.enter_context(nc.psum_tensor("pO0", [128, 512], F32))
            pO1 = _ps.enter_context(nc.psum_tensor("pO1", [128, 512], F32))
            pO2 = _ps.enter_context(nc.psum_tensor("pO2", [128, 512], F32))
            pO3 = _ps.enter_context(nc.psum_tensor("pO3", [128, 512], F32))
            pL0 = _ps.enter_context(nc.psum_tensor("pL0", [128, 512], F32))
            pL1 = _ps.enter_context(nc.psum_tensor("pL1", [128, 512], F32))
            blk = _ps.enter_context(nc.Block("phB"))
            pS = [pS0, pS1]
            pO = [pO0, pO1, pO2, pO3]
            pL = [pL0, pL1]

            @blk.sync
            def _(sy):
                sy.wait_ge(st_kt, 128 * NW)
                for dc in range(NDC // 2):
                    sy.dma_start(KT_res[:, dc * S:(dc + 1) * S],
                                 s_KT[dc, :, :]).then_inc(b_ldk, 16)
                for p in range(2):
                    npar = (NW + 1 - p) // 2
                    if npar:
                        sy.wait_ge(st_xpn[p], 64 * npar)
                        sy.wait_ge(st_xnt[p], 128 * npar)
                sy.wait_ge(st_qt, 128 * NW)
                sy.wait_ge(st_xnb, 64 * NW)
                for s in range(min(2, NQS)):
                    for dc in range(NDC):
                        sy.dma_start(qt_buf[:, (s % 2) * 2048 + dc * 256:
                                            (s % 2) * 2048 + (dc + 1) * 256],
                                     s_QT[dc, :, s * 256:(s + 1) * 256]
                                     ).then_inc(b_qt2[s % 2], 16)
                    for q in range(2):
                        i = 2 * s + q
                        sy.dma_start(xpn_buf[:, (s % 2) * 2048 + q * D:
                                             (s % 2) * 2048 + (q + 1) * D],
                                     s_xpn[i * 128:(i + 1) * 128, :]
                                     ).then_inc(b_xp2[s % 2], 16)
                        sy.dma_start(xw_buf[:, (s % 2) * 2048 + q * D:
                                            (s % 2) * 2048 + (q + 1) * D],
                                     x_h[i * 128:(i + 1) * 128, :]
                                     ).then_inc(b_x2[s % 2], 16)
                for s in range(NQS):
                    sy.wait_ge(b_vo2, s + 1)
                    for q in range(2):
                        i = 2 * s + q
                        sy.dma_start(s_pB[i * 128:(i + 1) * 128, :],
                                     xpn_buf[:, (s % 2) * 2048 + q * D:
                                             (s % 2) * 2048 + (q + 1) * D]
                                     ).then_inc(b_st, 16)
                    if s + 2 < NQS:
                        s2 = s + 2
                        sy.wait_ge(b_sht, NKT * (s + 1))
                        for dc in range(NDC):
                            sy.dma_start(qt_buf[:, (s2 % 2) * 2048 + dc * 256:
                                                (s2 % 2) * 2048 + (dc + 1) * 256],
                                         s_QT[dc, :, s2 * 256:(s2 + 1) * 256]
                                         ).then_inc(b_qt2[s2 % 2], 16)
                        sy.wait_ge(b_st, 32 * (s + 1))
                        for q in range(2):
                            i = 2 * s2 + q
                            sy.dma_start(xpn_buf[:, (s2 % 2) * 2048 + q * D:
                                                 (s2 % 2) * 2048 + (q + 1) * D],
                                         s_xpn[i * 128:(i + 1) * 128, :]
                                         ).then_inc(b_xp2[s2 % 2], 16)
                            sy.dma_start(xw_buf[:, (s2 % 2) * 2048 + q * D:
                                                (s2 % 2) * 2048 + (q + 1) * D],
                                         x_h[i * 128:(i + 1) * 128, :]
                                         ).then_inc(b_x2[s2 % 2], 16)

            @blk.gpsimd
            def _(g):
                g.wait_ge(st_kt, 128 * NW)
                for dc in range(NDC // 2, NDC):
                    g.dma_start(KT_res[:, dc * S:(dc + 1) * S],
                                s_KT[dc, :, :]).then_inc(b_ldk2, 16)
                for p in range(2):
                    npar = (NW + 1 - p) // 2
                    if npar:
                        g.wait_ge(st_xpn[p], 64 * npar)
                        g.wait_ge(st_xnt[p], 128 * npar)
                g.wait_ge(st_qt, 128 * NW)
                g.wait_ge(st_xnb, 64 * NW)
                g.wait_ge(st_v, 16 * NW)
                g.dma_start(V_res[:], s_V[:]).then_inc(b_ldv, 16)

            @blk.tensor
            def _(t_):
                t_.wait_ge(b_ldk, 16 * (NDC // 2))
                t_.wait_ge(b_ldk2, 16 * (NDC - NDC // 2))
                for s in range(NQS):
                    par = s % 2
                    t_.wait_ge(b_qt2[s % 2], 16 * NDC * (s // 2 + 1))
                    if s == 0:
                        t_.wait_ge(b_ldv, 16)
                    LAG = 3
                    for k in range(NKT + LAG):
                        if k < NKT:
                            K = NKT * s + k
                            if K >= 2:
                                t_.wait_ge(b_exp, K - 1)
                            for ec in range(NDC):
                                mm = t_.matmul(pS[k % 2][:, 0:256],
                                               KT_res[:, ec * S + k * 128:
                                                      ec * S + (k + 1) * 128],
                                               qt_buf[:, par * 2048 + ec * 256:
                                                      par * 2048 + (ec + 1) * 256],
                                               start=(ec == 0), stop=(ec == NDC - 1))
                            mm.then_inc(b_sht, 1)
                        j = k - LAG
                        if j >= 0:
                            KJ = NKT * s + j
                            if j == 0 and s >= 1:
                                t_.wait_ge(b_vo, s)
                            t_.wait_ge(b_exp, KJ + 1)
                            for q in range(2):
                                for h in range(2):
                                    t_.matmul(pO[2 * q + h][:, 0:512],
                                              pt_buf[:, (KJ % 4) * 256 + q * 128:
                                                     (KJ % 4) * 256 + (q + 1) * 128],
                                              V_res[:, j * D + h * 512:
                                                    j * D + (h + 1) * 512],
                                              start=(j == 0), stop=(j == NKT - 1))
                                mm = t_.matmul(pL[q][:, 0:1],
                                               pt_buf[:, (KJ % 4) * 256 + q * 128:
                                                      (KJ % 4) * 256 + (q + 1) * 128],
                                               ones_s[:, 0:1],
                                               start=(j == 0), stop=(j == NKT - 1))
                            mm.then_inc(b_pv, 1)

            @blk.scalar
            def _(s_):
                for s in range(NQS):
                    for k in range(NKT):
                        K = NKT * s + k
                        s_.wait_ge(b_sht, K + 1)
                        if K >= 4:
                            s_.wait_ge(b_pv, K - 3)
                        s_.activation(pt_buf[:, (K % 4) * 256:(K % 4) * 256 + 256],
                                      pS[k % 2][:, 0:256], Act.Exp,
                                      bias=cc(C_NEG8),
                                      scale=cc(C_1_32)).then_inc(b_exp, 1)

            @blk.vector
            def _(v):
                for s in range(NQS):
                    par = s % 2
                    v.wait_ge(b_pv, NKT * (s + 1))
                    v.wait_ge(b_xp2[s % 2], 32 * (s // 2 + 1))
                    v.drain()
                    v.reciprocal(linv[:, 0:1], pL[0][:, 0:1])
                    v.reciprocal(linv[:, 1:2], pL[1][:, 0:1])
                    v.drain()
                    for q in range(2):
                        for h in range(2):
                            dst = xpn_buf[:, par * 2048 + q * D + h * 512:
                                          par * 2048 + q * D + (h + 1) * 512]
                            stt = v.scalar_tensor_tensor(
                                dst, pO[2 * q + h][:, 0:512], linv[:, q:q + 1],
                                dst, op0=Alu.mult, op1=Alu.add)
                    stt.then_inc(b_vo, 1)
                    v.drain()
                    v.wait_ge(b_x2[s % 2], 32 * (s // 2 + 1))
                    for q in range(2):
                        dst = xpn_buf[:, par * 2048 + q * D:par * 2048 + (q + 1) * D]
                        tt = v.tensor_tensor(dst, dst,
                                             xw_buf[:, par * 2048 + q * D:
                                                    par * 2048 + (q + 1) * D],
                                             op=Alu.add)
                    tt.then_inc(b_vo2, 1)

        with contextlib.ExitStack() as _ps:
            slot_pi = _ps.enter_context(nc.sbuf_tensor("slot_pi", [128, NT], F32))
            slot_u = _ps.enter_context(nc.sbuf_tensor("slot_u", [128, NT], mybir.dt.uint32))
            perm_pi = _ps.enter_context(nc.sbuf_tensor("perm_pi", [128, NCT], F32))
            perm_u = _ps.enter_context(nc.sbuf_tensor("perm_u", [128, NCT], mybir.dt.uint32))
            wE = _ps.enter_context(nc.sbuf_tensor("wE", [128, 2 * NDC * D], BF16))
            xg = _ps.enter_context(nc.sbuf_tensor("xg", [128, 4 * D], BF16))
            xgT = _ps.enter_context(nc.sbuf_tensor("xgT", [128, 2 * D], BF16))
            y_sb = _ps.enter_context(nc.sbuf_tensor("y_sb", [128, 2 * D], F32))
            _setup_es = contextlib.ExitStack()
            ch8 = _setup_es.enter_context(nc.sbuf_tensor("ch8", [8, S], F32))
            mask8 = _setup_es.enter_context(nc.sbuf_tensor("mask8", [8, S], F32))
            incl8 = _setup_es.enter_context(nc.sbuf_tensor("incl8", [8, S], F32))
            rank_sb = _setup_es.enter_context(nc.sbuf_tensor("rank_sb", [1, S], F32))
            slot_row = _setup_es.enter_context(nc.sbuf_tensor("slot_row", [1, S], F32))
            pfb = _setup_es.enter_context(nc.sbuf_tensor("pfb", [CAPTOT // 128, 128], F32))
            tr1_sb = _setup_es.enter_context(nc.sbuf_tensor("tr1_sb", [NT, 128], F32))
            tr2_sb = _setup_es.enter_context(nc.sbuf_tensor("tr2_sb", [NT, 128], F32))
            pp_sb = _setup_es.enter_context(nc.sbuf_tensor("pp_sb", [CAPTOT // 128, 128], F32))
            _setup_es.close()
            pg = _ps.enter_context(nc.sbuf_tensor("pg", [128, 4 * D], F32))
            cmask = _ps.enter_context(nc.sbuf_tensor("cmask", [128, 1], F32))
            pCt0 = _ps.enter_context(nc.psum_tensor("pCt0", [128, 1024], BF16))
            pCt1 = _ps.enter_context(nc.psum_tensor("pCt1", [128, 1024], BF16))
            pC4 = _ps.enter_context(nc.psum_tensor("pC4", [128, 512], F32))
            pC5 = _ps.enter_context(nc.psum_tensor("pC5", [128, 512], F32))
            pC6 = _ps.enter_context(nc.psum_tensor("pC6", [128, 512], F32))
            pC7 = _ps.enter_context(nc.psum_tensor("pC7", [128, 512], F32))
            blk = _ps.enter_context(nc.Block("phC"))
            pCt = [pCt0, pCt1]
            pCy = [pC4, pC5, pC6, pC7]
            pR = [pC4, pC5]

            NRK = S // 512

            @blk.sync
            def _(sy):
                sy.wait_ge(b_st, 32 * NQS)
                sy.wait_ge(c_e1, 1)
                sy.dma_start(s_m1[:, :], tr1_sb[:]).then_inc(c_s1, 16)
                sy.wait_ge(c_s1, 16)
                m1_t = bass.AP(s_m1.tensor, 0, [[S, 1], [1, S]])
                for r in range(8):
                    sy.dma_start(ch8[r:r + 1, :], m1_t).then_inc(c_s2, 16)
                sy.wait_ge(c_v2, 1)
                sy.dma_start(s_m2[0:1, :], slot_row[0:1, :]).then_inc(c_s3, 16)
                sy.wait_ge(c_s3, 16)
                m2v = bass.AP(s_m2.tensor, 0, [[128, NT], [1, 128]])
                sy.dma_start(tr2_sb[:], m2v).then_inc(c_s4, 16)
                sy.wait_ge(c_gs, 16 * NT)
                sy.dma_start(pp_sb[:], s_perm).then_inc(c_s5, 16)

            @blk.gpsimd
            def _(g_):
                g_.wait_ge(b_st, 32 * NQS)
                g_.dma_start(pfb[:], dumpf_h).then_inc(c_gp, 16)
                g_.wait_ge(c_gp, 16)
                g_.dma_start(s_perm, pfb[:]).then_inc(c_gp, 16)
                g_.wait_ge(c_gp, 32)
                g_.wait_ge(c_v3, 1)
                pflat = bass.AP(s_perm.tensor, 0, [[CAPTOT, 1], [1, CAPTOT]])
                for i in range(NT):
                    g_.indirect_dma_start(
                        pflat, bass.IndirectOffsetOnAxis(slot_u[:, i:i + 1], 1),
                        iotat_s[:, i:i + 1], None).then_inc(c_gs, 16)
                def exp_load(e):
                    if e >= 2:
                        g_.wait_ge(c_my, CAPT * (e - 1))
                    for dc in range(NDC):
                        g_.dma_start(wE[:, (e % 2) * NDC * D + dc * D:
                                        (e % 2) * NDC * D + (dc + 1) * D],
                                     ewT_h[e, dc * 128:(dc + 1) * 128, :]
                                     ).then_inc(c_wl[e % 2], 16)

                exp_load(0)
                exp_load(1)
                g_.wait_ge(st_xnb, 64 * NW)
                g_.wait_ge(c_v4, 1)
                g_.wait_ge(c_z, 16)
                _ins = {CAPT * (e - 1) + max(1, CAPT // 2): e
                        for e in range(2, E)}
                bnd_r = g_.alloc_register(name="bnd")
                g_.reg_mov(bnd_r, S - 1)

                def out_scat(ct):
                    g_.wait_ge(c_oa, ct + 1)
                    g_.indirect_dma_start(
                        out_h, bass.IndirectOffsetOnAxis(perm_u[:, ct:ct + 1], 0),
                        pg[:, (ct % 4) * D:(ct % 4 + 1) * D], None,
                        bounds_check=bnd_r, oob_is_err=False
                    ).then_inc(c_os[ct % 4], 16)

                for ct in range(NCT):
                    if ct in _ins:
                        exp_load(_ins[ct])
                    if ct >= 4:
                        g_.wait_ge(c_tp2, ct - 3)
                    g_.indirect_dma_start(
                        xg[:, (ct % 4) * D:(ct % 4 + 1) * D], None,
                        s_xnb, bass.IndirectOffsetOnAxis(perm_u[:, ct:ct + 1], 0)
                    ).then_inc(c_g2[ct % 4], 16)
                    if ct >= 4:
                        g_.wait_ge(c_os[ct % 4], 16 * ((ct - 4) // 4 + 1))
                    g_.indirect_dma_start(
                        pg[:, (ct % 4) * D:(ct % 4 + 1) * D], None,
                        s_pB, bass.IndirectOffsetOnAxis(perm_u[:, ct:ct + 1], 0)
                    ).then_inc(c_pg[ct % 4], 16)
                    if ct >= 2:
                        out_scat(ct - 2)
                out_scat(NCT - 2)
                out_scat(NCT - 1)

            @blk.vector
            def _(v):
                v.wait_ge(c_t1, 1)
                v.tensor_copy(tr1_sb[:], pR[0][0:NT, 0:128]).then_inc(c_e1, 1)
                v.wait_ge(c_s2, 128)
                v.tensor_scalar(mask8[:], ch8[:], iotac_s[0:8, 0:1], None,
                                op0=Alu.is_equal)
                v.drain()
                v.tensor_tensor_scan(incl8[:], mask8[:], mask8[:], 0.0,
                                     op0=Alu.add, op1=Alu.bypass)
                v.drain()
                v.tensor_tensor(mask8[:], mask8[:], incl8[:],
                                op=Alu.mult).then_inc(c_v1, 1)
                for j in range(NRK):
                    v.wait_ge(c_mr, j + 1)
                    cp = v.tensor_copy(rank_sb[0:1, j * 512:(j + 1) * 512],
                                       pR[j % 2][0:1, 0:512])
                    cp.then_inc(c_rv, 1)
                v.drain()
                v.tensor_scalar(slot_row[0:1, :], ch8[0:1, :], float(CAP), -1.0,
                                op0=Alu.mult, op1=Alu.add)
                v.drain()
                v.tensor_tensor(slot_row[0:1, :], slot_row[0:1, :], rank_sb[0:1, :],
                                op=Alu.add)
                v.drain()
                v.tensor_scalar(slot_row[0:1, :], slot_row[0:1, :],
                                float(CAPTOT - 1), None,
                                op0=Alu.min).then_inc(c_v2, 1)
                v.wait_ge(c_t2, 1)
                v.tensor_copy(slot_pi[:], pR[0][:, 0:NT])
                v.drain()
                v.tensor_copy(slot_u[:], slot_pi[:]).then_inc(c_v3, 1)
                v.wait_ge(c_t3, 1)
                v.tensor_copy(perm_pi[:], pR[1][:, 0:NCT])
                v.drain()
                v.tensor_copy(perm_u[:], perm_pi[:]).then_inc(c_v4, 1)
                for ct in range(NCT):
                    v.wait_ge(c_ye, ct + 1)
                    v.wait_ge(c_pg[ct % 4], 16 * (ct // 4 + 1))
                    v.drain()
                    dst = pg[:, (ct % 4) * D:(ct % 4 + 1) * D]
                    v.tensor_tensor(dst, dst,
                                    y_sb[:, (ct % 2) * D:(ct % 2 + 1) * D],
                                    op=Alu.add).then_inc(c_oa, 1)

            @blk.scalar
            def _(s_):
                def s_xgt(ct):
                    s_.wait_ge(c_tp2, ct + 1)
                    if ct >= 2:
                        s_.wait_ge(c_my, ct - 1)
                    aa = s_.activation(
                        xgT[:, (ct % 2) * D:(ct % 2 + 1) * D],
                        pCt[ct % 2][:, :], Act.Identity,
                        bias=cc(C_ZERO), scale=cc(C_ONE))
                    aa.then_inc(c_xe, 1)

                s_xgt(0)
                for ct in range(NCT):
                    if ct + 1 < NCT:
                        s_xgt(ct + 1)
                    s_.wait_ge(c_my, ct + 1)
                    if ct >= 2:
                        s_.wait_ge(c_oa, ct - 1)
                    for hh in range(2):
                        aa = s_.activation(
                            y_sb[:, (ct % 2) * D + hh * 512:(ct % 2) * D + (hh + 1) * 512],
                            pCy[2 * (ct % 2) + hh][:, 0:512], Act.Identity,
                            bias=cc(C_ZERO), scale=cc(C_ONE))
                    aa.then_inc(c_ye, 1)
                for p in range(4):
                    n = (NCT - p + 3) // 4
                    if n:
                        s_.wait_ge(c_os[p], 16 * n)

            @blk.tensor
            def _(t_):
                t_.transpose(pR[0][0:NT, 0:128], idx_res[:], ident[:]).then_inc(c_t1, 1)
                t_.wait_ge(c_v1, 1)
                for j in range(NRK):
                    if j == 0:
                        t_.wait_ge(c_e1, 1)
                    if j >= 2:
                        t_.wait_ge(c_rv, j - 1)
                    t_.matmul(pR[j % 2][0:1, 0:512], consts[0:8, C_ONE:C_ONE + 1],
                              mask8[:, j * 512:(j + 1) * 512],
                              start=True, stop=True).then_inc(c_mr, 1)
                t_.wait_ge(c_rv, NRK)
                t_.wait_ge(c_s4, 16)
                t_.transpose(pR[0][:, 0:NT], tr2_sb[:],
                             ident[0:NT, 0:NT]).then_inc(c_t2, 1)
                t_.wait_ge(c_s5, 16)
                t_.transpose(pR[1][:, 0:NCT], pp_sb[:],
                             ident[0:CAPTOT // 128, 0:CAPTOT // 128]).then_inc(c_t3, 1)
                t_.wait_ge(c_v3, 1)
                t_.wait_ge(c_v4, 1)

                def tp2(ct):
                    t_.wait_ge(c_g2[ct % 4], 16 * (ct // 4 + 1))
                    if ct >= 2:
                        t_.wait_ge(c_xe, ct - 1)
                    for dc in range(8):
                        tr = t_.transpose(
                            pCt[ct % 2][:, dc * 128:(dc + 1) * 128],
                            xg[:, (ct % 4) * D + dc * 128:(ct % 4) * D + (dc + 1) * 128],
                            identb[:])
                        if dc == 7:
                            tr.then_inc(c_tp2, 1)

                tp2(0)
                tp2(1)
                for ct in range(NCT):
                    e = ct // CAPT
                    t_.wait_ge(c_wl[e % 2], 16 * NDC * (e // 2 + 1))
                    t_.wait_ge(c_xe, ct + 1)
                    if ct >= 2:
                        t_.wait_ge(c_ye, ct - 1)
                    for dc in range(NDC):
                        for hh in range(2):
                            mm = t_.matmul(
                                pCy[2 * (ct % 2) + hh][:, 0:512],
                                xgT[:, (ct % 2) * D + dc * 128:(ct % 2) * D + (dc + 1) * 128],
                                wE[:, (e % 2) * NDC * D + dc * D + hh * 512:
                                   (e % 2) * NDC * D + dc * D + (hh + 1) * 512],
                                start=(dc == 0), stop=(dc == NDC - 1))
                    mm.then_inc(c_my, 1)
                    if ct + 2 < NCT:
                        tp2(ct + 2)

    return nc


def _host_inputs(inputs, S=4096):
    f32 = np.float32
    Wq = np.asarray(inputs["L0_attn_Wq"], f32)
    Wk = np.asarray(inputs["L0_attn_Wk"], f32)
    Wv = np.asarray(inputs["L0_attn_Wv"], f32)
    bq = np.asarray(inputs["L0_attn_bq"], f32)
    bk = np.asarray(inputs["L0_attn_bk"], f32)
    bv = np.asarray(inputs["L0_attn_bv"], f32)
    gw = np.asarray(inputs["L0_ffn_gate_w"], f32)
    ew = np.asarray(inputs["L0_ffn_expert_w"], f32)
    shared = {
        "wqT": np.ascontiguousarray(Wq.T),
        "wkT": np.ascontiguousarray(Wk.T),
        "wvT": np.ascontiguousarray(Wv.T),
        "ewT": np.ascontiguousarray(ew.transpose(0, 2, 1)),
        "gwT": np.ascontiguousarray(gw.reshape(8, 8, 128).transpose(2, 1, 0)
                                    .reshape(128, 64)),
        "bqc": np.ascontiguousarray(bq.reshape(8, 128).T),
        "bkc": np.ascontiguousarray(bk.reshape(8, 128).T),
        "bvb": np.ascontiguousarray(np.broadcast_to(bv, (128, D))),
        "cst": np.ascontiguousarray(np.broadcast_to(
            np.array(CONST_VALS + [0.0], f32), (128, 8))),
        "idm": np.eye(128, dtype=f32),
        "idmb": np.eye(128, dtype=ml_dtypes.bfloat16),
        "onesb": np.ones((128, 1), dtype=ml_dtypes.bfloat16),
        "iotac": np.arange(128, dtype=f32).reshape(128, 1),
        "iotat": np.ascontiguousarray(
            (np.arange(S // 128, dtype=f32)[None, :] * 128
             + np.arange(128, dtype=f32)[:, None])),
        "dumpf": np.full((2 * S // 128, 128), float(S), dtype=f32),
        "zrow": np.zeros((1, D), dtype=ml_dtypes.bfloat16),
        "zrowf": np.zeros((1, D), dtype=np.float32),
    }
    return shared


_NC_CACHE = {}


def kernel(L0_attn_x, L0_attn_Wq, L0_attn_bq, L0_attn_Wk, L0_attn_bk,
           L0_attn_Wv, L0_attn_bv, L0_ffn_gate_w, L0_ffn_expert_w):
    S = 4096
    inputs = dict(L0_attn_x=L0_attn_x, L0_attn_Wq=L0_attn_Wq, L0_attn_bq=L0_attn_bq,
                  L0_attn_Wk=L0_attn_Wk, L0_attn_bk=L0_attn_bk, L0_attn_Wv=L0_attn_Wv,
                  L0_attn_bv=L0_attn_bv, L0_ffn_gate_w=L0_ffn_gate_w,
                  L0_ffn_expert_w=L0_ffn_expert_w)
    shared = _host_inputs(inputs, S)
    x = np.asarray(L0_attn_x, np.float32)
    in_maps = []
    for b in range(8):
        m = dict(shared)
        m["x"] = np.ascontiguousarray(x[b])
        in_maps.append(m)
    if S not in _NC_CACHE:
        _NC_CACHE[S] = build_nc(S)
    nc = _NC_CACHE[S]
    res = run_bass_kernel_spmd(nc, in_maps, core_ids=list(range(8)))
    out = np.stack([np.asarray(r["out"], np.float32)[:S] for r in res.results])
    return out


# version 29 (final)
# speedup vs baseline: 1.0865x; 1.0865x over previous
import numpy as np
import ml_dtypes
import concourse.bass as bass
import concourse.mybir as mybir
from concourse.bass_utils import run_bass_kernel_spmd

F32 = mybir.dt.float32
BF16 = mybir.dt.bfloat16
Act = mybir.ActivationFunctionType
Alu = mybir.AluOpType
AX = mybir.AxisListType.X

D = 1024
NDC = 8
E = 8
C_ONE, C_ZERO, C_EPS, C_INV1024, C_NEG05, C_1_32, C_NEG8 = range(7)
CONST_VALS = [1.0, 0.0, 1e-5, 1.0 / 1024.0, -0.5, 1.0 / 32.0, -8.0]


def build_nc(S=4096):
    NT = S // 128
    NT2 = NT // 2
    NW = S // 512
    NQS = S // 256
    NKT = S // 128
    NK = 32 * NQS
    NST = S // 512
    CAPT = 2 * (S // 128) // E
    CAP = CAPT * 128
    CAPTOT = E * CAP
    NCT = E * CAPT

    nc = bass.Bass("TRN2", target_bir_lowering=False)

    x_h = nc.dram_tensor("x", [S, D], F32, kind="ExternalInput").ap()
    wqT_h = nc.dram_tensor("wqT", [D, D], F32, kind="ExternalInput").ap()
    wkT_h = nc.dram_tensor("wkT", [D, D], F32, kind="ExternalInput").ap()
    wvT_h = nc.dram_tensor("wvT", [D, D], F32, kind="ExternalInput").ap()
    ewT_h = nc.dram_tensor("ewT", [E, D, D], F32, kind="ExternalInput").ap()
    gw_h = nc.dram_tensor("gwT", [128, NDC * E], F32, kind="ExternalInput").ap()
    bqc_h = nc.dram_tensor("bqc", [128, 8], F32, kind="ExternalInput").ap()
    bkc_h = nc.dram_tensor("bkc", [128, 8], F32, kind="ExternalInput").ap()
    bvb_h = nc.dram_tensor("bvb", [128, D], F32, kind="ExternalInput").ap()
    cst_h = nc.dram_tensor("cst", [128, 8], F32, kind="ExternalInput").ap()
    idm_h = nc.dram_tensor("idm", [128, 128], F32, kind="ExternalInput").ap()
    idmb_h = nc.dram_tensor("idmb", [128, 128], BF16, kind="ExternalInput").ap()
    ones_h = nc.dram_tensor("onesb", [128, 1], BF16, kind="ExternalInput").ap()
    iotac_h = nc.dram_tensor("iotac", [128, 1], F32, kind="ExternalInput").ap()
    iotat_h = nc.dram_tensor("iotat", [128, NT], F32, kind="ExternalInput").ap()
    dumpf_h = nc.dram_tensor("dumpf", [CAPTOT // 128, 128], F32, kind="ExternalInput").ap()
    zrow_h = nc.dram_tensor("zrow", [1, D], BF16, kind="ExternalInput").ap()
    zrowf_h = nc.dram_tensor("zrowf", [1, D], F32, kind="ExternalInput").ap()
    out_h = nc.dram_tensor("out", [S, D], F32, kind="ExternalOutput").ap()

    s_xnT = nc.dram_tensor("s_xnT", [NDC, 128, S], BF16, kind="Internal").ap()
    s_QT = nc.dram_tensor("s_QT", [NDC, 128, S], BF16, kind="Internal").ap()
    s_KT = nc.dram_tensor("s_KT", [NDC, 128, S], BF16, kind="Internal").ap()
    s_V = nc.dram_tensor("s_V", [128, NT * D], BF16, kind="Internal").ap()
    s_xpn = nc.dram_tensor("s_xpn", [S, D], F32, kind="Internal").ap()
    s_pB = nc.dram_tensor("s_pB", [S + 1, D], F32, kind="Internal").ap()
    s_xnb = nc.dram_tensor("s_xnb", [S + 1, D], BF16, kind="Internal").ap()
    s_perm = nc.dram_tensor("s_perm", [CAPTOT // 128, 128], F32, kind="Internal").ap()
    s_m1 = nc.dram_tensor("s_m1", [NT, 128], F32, kind="Internal").ap()
    s_m2 = nc.dram_tensor("s_m2", [1, S], F32, kind="Internal").ap()

    import contextlib
    with contextlib.ExitStack() as _es:
        consts = _es.enter_context(nc.sbuf_tensor("consts", [128, 8], F32))
        ident = _es.enter_context(nc.sbuf_tensor("ident", [128, 128], F32))
        ones_s = _es.enter_context(nc.sbuf_tensor("ones_s", [128, 1], BF16))
        idx_res = _es.enter_context(nc.sbuf_tensor("idx_res", [128, NT], F32))
        identb = _es.enter_context(nc.sbuf_tensor("identb", [128, 128], BF16))
        iotac_s = _es.enter_context(nc.sbuf_tensor("iotac_s", [128, 1], F32))
        iotat_s = _es.enter_context(nc.sbuf_tensor("iotat_s", [128, NT], F32))
        a_pre = _es.enter_context(nc.semaphore("a_pre"))
        a_w = _es.enter_context(nc.semaphore("a_w"))
        a_x = _es.enter_context(nc.semaphore("a_x"))
        a_vs = _es.enter_context(nc.semaphore("a_vs"))
        a_sx = _es.enter_context(nc.semaphore("a_sx"))
        a_xn = _es.enter_context(nc.semaphore("a_xn"))
        a_xpn = _es.enter_context(nc.semaphore("a_xpn"))
        a_tp = _es.enter_context(nc.semaphore("a_tp"))
        a_evs = _es.enter_context(nc.semaphore("a_evs"))
        a_evv = _es.enter_context(nc.semaphore("a_evv"))
        a_mmq = _es.enter_context(nc.semaphore("a_mmq"))
        a_evq = _es.enter_context(nc.semaphore("a_evq"))
        a_mmk = _es.enter_context(nc.semaphore("a_mmk"))
        a_evk = _es.enter_context(nc.semaphore("a_evk"))
        a_mmv = _es.enter_context(nc.semaphore("a_mmv"))
        a_evV = _es.enter_context(nc.semaphore("a_evV"))
        a_mmg = _es.enter_context(nc.semaphore("a_mmg"))
        a_evg = _es.enter_context(nc.semaphore("a_evg"))
        a_gt = _es.enter_context(nc.semaphore("a_gt"))
        st_xpn = [_es.enter_context(nc.semaphore(f"st_xpn{p}")) for p in range(2)]
        st_xnt = [_es.enter_context(nc.semaphore(f"st_xnt{p}")) for p in range(2)]
        st_qt = _es.enter_context(nc.semaphore("st_qt"))
        st_kt = _es.enter_context(nc.semaphore("st_kt"))
        st_v = _es.enter_context(nc.semaphore("st_v"))
        st_xnb = _es.enter_context(nc.semaphore("st_xnb"))
        a_xnb = _es.enter_context(nc.semaphore("a_xnb"))
        a_xp2 = [_es.enter_context(nc.semaphore(f"a_xp{p}")) for p in range(2)]
        b_ldk = _es.enter_context(nc.semaphore("b_ldk"))
        b_ldk2 = _es.enter_context(nc.semaphore("b_ldk2"))
        b_ldv = _es.enter_context(nc.semaphore("b_ldv"))
        b_qt2 = [_es.enter_context(nc.semaphore(f"b_qt{p}")) for p in range(2)]
        b_xp2 = [_es.enter_context(nc.semaphore(f"b_xp{p}")) for p in range(2)]
        b_x2 = [_es.enter_context(nc.semaphore(f"b_x{p}")) for p in range(2)]
        b_vo2 = _es.enter_context(nc.semaphore("b_vo2"))
        b_sht = _es.enter_context(nc.semaphore("b_sht"))
        b_exp = _es.enter_context(nc.semaphore("b_exp"))
        b_pv = _es.enter_context(nc.semaphore("b_pv"))
        b_vo = _es.enter_context(nc.semaphore("b_vo"))
        b_st = _es.enter_context(nc.semaphore("b_st"))
        c_s1 = _es.enter_context(nc.semaphore("c_s1"))
        c_t1 = _es.enter_context(nc.semaphore("c_t1"))
        c_e1 = _es.enter_context(nc.semaphore("c_e1"))
        c_t2 = _es.enter_context(nc.semaphore("c_t2"))
        c_t3 = _es.enter_context(nc.semaphore("c_t3"))
        c_s2 = _es.enter_context(nc.semaphore("c_s2"))
        c_s3 = _es.enter_context(nc.semaphore("c_s3"))
        c_s4 = _es.enter_context(nc.semaphore("c_s4"))
        c_s5 = _es.enter_context(nc.semaphore("c_s5"))
        c_v1 = _es.enter_context(nc.semaphore("c_v1"))
        c_v2 = _es.enter_context(nc.semaphore("c_v2"))
        c_v3 = _es.enter_context(nc.semaphore("c_v3"))
        c_v4 = _es.enter_context(nc.semaphore("c_v4"))
        c_mr = _es.enter_context(nc.semaphore("c_mr"))
        c_rv = _es.enter_context(nc.semaphore("c_rv"))
        c_gp = _es.enter_context(nc.semaphore("c_gp"))
        c_gs = _es.enter_context(nc.semaphore("c_gs"))
        c_g2 = [_es.enter_context(nc.semaphore(f"c_g{p}")) for p in range(4)]
        c_wl = [_es.enter_context(nc.semaphore(f"c_wl{p}")) for p in range(2)]
        c_tp2 = _es.enter_context(nc.semaphore("c_tp2"))
        c_xe = _es.enter_context(nc.semaphore("c_xe"))
        c_my = _es.enter_context(nc.semaphore("c_my"))
        c_ye = _es.enter_context(nc.semaphore("c_ye"))
        c_pg = [_es.enter_context(nc.semaphore(f"c_pg{p}")) for p in range(8)]
        c_os = [_es.enter_context(nc.semaphore(f"c_os{p}")) for p in range(8)]
        c_oa = _es.enter_context(nc.semaphore("c_oa"))
        c_z = _es.enter_context(nc.semaphore("c_z"))
        for j, v in enumerate(CONST_VALS):
            nc.const_aps.register(F32, v, consts[:, j:j + 1])

        def cc(j):
            return consts[:, j:j + 1]

        with contextlib.ExitStack() as _ps:
            wq_b = _ps.enter_context(nc.sbuf_tensor("wq_b", [128, NDC * D], BF16))
            wk_b = _ps.enter_context(nc.sbuf_tensor("wk_b", [128, NDC * D], BF16))
            wv_b = _ps.enter_context(nc.sbuf_tensor("wv_b", [128, NDC * D], BF16))
            gw_s = _ps.enter_context(nc.sbuf_tensor("gw_s", [128, NDC * E], F32))
            bqc_s = _ps.enter_context(nc.sbuf_tensor("bqc_s", [128, 8], F32))
            bkc_s = _ps.enter_context(nc.sbuf_tensor("bkc_s", [128, 8], F32))
            bvb_s = _ps.enter_context(nc.sbuf_tensor("bvb_s", [128, D], F32))
            xwin = _ps.enter_context(nc.sbuf_tensor("xwin", [128, 2 * 4 * D], F32))
            xnwin = _ps.enter_context(nc.sbuf_tensor("xnwin", [128, 2 * 4 * D], F32))
            sqdump = _ps.enter_context(nc.sbuf_tensor("sqdump", [128, D], F32))
            stats = _ps.enter_context(nc.sbuf_tensor("stats", [128, 4 * 8], F32))
            xnb_sb = _ps.enter_context(nc.sbuf_tensor("xnb_sb", [128, 4 * D], BF16))
            xnT_b = _ps.enter_context(nc.sbuf_tensor("xnT_b", [128, 2 * NDC * 512], BF16))
            xnT_f = _ps.enter_context(nc.sbuf_tensor("xnT_f", [128, 2 * NDC * 512], F32))
            qt_sb = _ps.enter_context(nc.sbuf_tensor("qt_sb", [128, NDC * 512], BF16))
            kt_sb = _ps.enter_context(nc.sbuf_tensor("kt_sb", [128, NDC * 512], BF16))
            vt_sb = _ps.enter_context(nc.sbuf_tensor("vt_sb", [128, 4 * D], BF16))
            gT_sb = _ps.enter_context(nc.sbuf_tensor("gT_sb", [128, 512], F32))
            gtok = _ps.enter_context(nc.sbuf_tensor("gtok", [128, 8], F32))
            gmax = _ps.enter_context(nc.sbuf_tensor("gmax", [128, 8], F32))
            gidx = _ps.enter_context(nc.sbuf_tensor("gidx", [128, 8], mybir.dt.uint32))
            pT0 = _ps.enter_context(nc.psum_tensor("pT0", [128, 512], F32))
            pT1 = _ps.enter_context(nc.psum_tensor("pT1", [128, 512], F32))
            pQK0 = _ps.enter_context(nc.psum_tensor("pQK0", [128, 512], F32))
            pQK1 = _ps.enter_context(nc.psum_tensor("pQK1", [128, 512], F32))
            pV0 = _ps.enter_context(nc.psum_tensor("pV0", [128, 512], F32))
            pV1 = _ps.enter_context(nc.psum_tensor("pV1", [128, 512], F32))
            pV2 = _ps.enter_context(nc.psum_tensor("pV2", [128, 512], F32))
            pV3 = _ps.enter_context(nc.psum_tensor("pV3", [128, 512], F32))
            blk = _ps.enter_context(nc.Block("phA"))
            pT = [pT0, pT1]
            pQK = [pQK0, pQK1]
            pV = [pV0, pV1, pV2, pV3]
            xnTb_v = xnT_b[:].rearrange("p (a c t s) -> p a c t s", a=2, c=NDC, t=4)
            xnTf_v = xnT_f[:].rearrange("p (a c t s) -> p a c t s", a=2, c=NDC, t=4)
            pT0_v = pT0[:].rearrange("p (c s) -> p c s", c=4)
            pT1_v = pT1[:].rearrange("p (c s) -> p c s", c=4)

            @blk.sync
            def _(sy):
                sy.dma_start(consts[:], cst_h).then_inc(a_pre, 16)
                sy.dma_start(ident[:], idm_h).then_inc(a_pre, 16)
                sy.dma_start(ones_s[:], ones_h).then_inc(a_pre, 16)
                sy.dma_start(gw_s[:], gw_h).then_inc(a_pre, 16)
                sy.dma_start(bqc_s[:], bqc_h).then_inc(a_pre, 16)
                sy.dma_start(bkc_s[:], bkc_h).then_inc(a_pre, 16)
                sy.dma_start(bvb_s[:], bvb_h).then_inc(a_pre, 16)
                sy.dma_start(identb[:], idmb_h).then_inc(a_pre, 16)
                sy.dma_start(iotac_s[:], iotac_h).then_inc(a_pre, 16)
                sy.dma_start(iotat_s[:], iotat_h).then_inc(a_pre, 16)
                sy.dma_start(s_xnb[S:S + 1, :], zrow_h).then_inc(a_pre, 16)
                sy.dma_start(s_pB[S:S + 1, :], zrowf_h).then_inc(c_z, 16)
                for w in range(NW):
                    for t in range(4):
                        i = 4 * w + t
                        sy.wait_ge(a_xn, i + 1)
                        sy.dma_start(s_xpn[i * 128:(i + 1) * 128, :],
                                     xnwin[:, (w % 2) * 4 * D + t * D:
                                           (w % 2) * 4 * D + (t + 1) * D]
                                     ).then_inc(st_xpn[w % 2], 16)
                    for t in range(4):
                        i = 4 * w + t
                        sy.wait_ge(a_xnb, i + 1)
                        sy.dma_start(s_xnb[i * 128:(i + 1) * 128, :],
                                     xnb_sb[:, t * D:(t + 1) * D]
                                     ).then_inc(st_xnb, 16)
                    sy.wait_ge(a_evs, 4 * w + 4)
                    sy.wait_ge(a_evv, 4 * w + 4)
                    for dc in range(NDC):
                        sy.dma_start(
                            s_xnT[dc, :, w * 512:(w + 1) * 512],
                            xnT_b[:, (w % 2) * 4096 + dc * 512:
                                  (w % 2) * 4096 + (dc + 1) * 512]
                        ).then_inc(st_xnt[w % 2], 16)
                    for ec in range(NDC):
                        sy.wait_ge(a_evq, 8 * w + ec + 1)
                        sy.dma_start(s_QT[ec, :, w * 512:(w + 1) * 512],
                                     qt_sb[:, ec * 512:(ec + 1) * 512]
                                     ).then_inc(st_qt, 16)
                    for ec in range(NDC):
                        sy.wait_ge(a_evk, 8 * w + ec + 1)
                        sy.dma_start(s_KT[ec, :, w * 512:(w + 1) * 512],
                                     kt_sb[:, ec * 512:(ec + 1) * 512]
                                     ).then_inc(st_kt, 16)
                    sy.wait_ge(a_evV, 4 * w + 4)
                    sy.dma_start(s_V[:, w * 4 * D:(w + 1) * 4 * D],
                                 vt_sb[:]).then_inc(st_v, 16)

            @blk.gpsimd
            def _(g):
                wjobs = [(wq_b, wqT_h), (wk_b, wkT_h), (wv_b, wvT_h)]
                def load_x(w):
                    if w >= 2:
                        g.wait_ge(a_xn, 4 * (w - 1))
                    for t in range(4):
                        i = 4 * w + t
                        g.dma_start(xwin[:, (w % 2) * 4 * D + t * D:
                                         (w % 2) * 4 * D + (t + 1) * D],
                                    x_h[i * 128:(i + 1) * 128, :]).then_inc(a_xp2[w % 2], 16)
                def load_w(j):
                    wsb, wh = wjobs[j]
                    for dc in range(NDC):
                        g.dma_start(wsb[:, dc * D:(dc + 1) * D],
                                    wh[dc * 128:(dc + 1) * 128, :]).then_inc(a_w, 16)
                load_x(0)
                load_w(0)
                if NW > 1:
                    load_x(1)
                load_w(1)
                load_w(2)
                for w in range(2, NW):
                    load_x(w)

            @blk.vector
            def _(v):
                v.wait_ge(a_pre, 176)
                v.memset(gmax[:], 0)
                v.drain()

                def v_stat1(w, t):
                    par = w % 2
                    i = 4 * w + t
                    xt = xwin[:, par * 4 * D + t * D:par * 4 * D + (t + 1) * D]
                    xn = xnwin[:, par * 4 * D + t * D:par * 4 * D + (t + 1) * D]
                    ss = stats[:, t * 8:t * 8 + 8]
                    if t == 0:
                        v.wait_ge(a_xp2[w % 2], 64 * (w // 2 + 1))
                    v.tensor_reduce(ss[:, 0:1], xt, axis=AX, op=Alu.add)
                    v.drain()
                    v.tensor_scalar(ss[:, 1:2], ss[:, 0:1], -1.0 / 1024.0, None,
                                    op0=Alu.mult).then_inc(a_vs, 1)
                    v.wait_ge(a_sx, i + 1)
                    v.tensor_tensor(ss[:, 5:6], ss[:, 1:2], ss[:, 4:5],
                                    op=Alu.mult)
                    v.drain()
                    if i >= 8:
                        v.wait_ge(a_tp, 2 * (i - 8) + 2)
                        v.wait_ge(st_xpn[par], 64 * (w // 2))
                        v.wait_ge(a_xnb, i - 4)
                    v.tensor_scalar(xn, xt, ss[:, 4:5], ss[:, 5:6],
                                    op0=Alu.mult, op1=Alu.add).then_inc(a_xn, 1)

                def v_kev(w, ec):
                    v.wait_ge(a_mmk, 8 * w + ec + 1)
                    if ec == 0 and w >= 1:
                        v.wait_ge(st_kt, 128 * w)
                    v.tensor_scalar(kt_sb[:, ec * 512:(ec + 1) * 512],
                                    pQK[ec % 2][:, 0:512], bkc_s[:, ec:ec + 1],
                                    None, op0=Alu.add).then_inc(a_evk, 1)

                def v_pt1_1(w, t):
                    par = w % 2
                    i = 4 * w + t
                    v.wait_ge(a_tp, 2 * i + 2)
                    if w >= 2:
                        v.wait_ge(a_mmg, w - 1)
                        v.wait_ge(st_xnt[w % 2], 128 * (w // 2))
                    v.tensor_copy(xnTb_v[:, par, 4:8, t, :], pT1_v)
                    v.tensor_copy(xnTf_v[:, par, 4:8, t, :],
                                  pT1_v).then_inc(a_evv, 1)

                def v_pt1(w):
                    for t in range(4):
                        v_pt1_1(w, t)

                def v_post(w):
                    for t in range(4):
                        i = 4 * w + t
                        v.wait_ge(a_mmv, i + 1)
                        if w >= 1 and t == 0:
                            v.wait_ge(st_v, 16 * w)
                        for h in range(2):
                            tt = v.tensor_tensor(
                                vt_sb[:, t * D + h * 512:t * D + (h + 1) * 512],
                                pV[2 * (t % 2) + h][:, 0:512],
                                bvb_s[:, h * 512:(h + 1) * 512], op=Alu.add)
                            if h == 1:
                                tt.then_inc(a_evV, 1)
                    v.wait_ge(a_mmg, w + 1)
                    v.tensor_copy(gT_sb[0:8, :], pQK[0][0:8, :]).then_inc(a_evg, 1)
                    v.wait_ge(a_gt, w + 1)
                    for t in range(4):
                        i = 4 * w + t
                        v.drain()
                        v.tensor_copy(gtok[:], pQK[1][:, t * 8:t * 8 + 8])
                        v.drain()
                        v.tensor_reduce(gmax[:, 0:1], gtok[:], axis=AX, op=Alu.max)
                        v.drain()
                        v.max_index(gidx[:], gmax[:], gtok[:])
                        v.drain()
                        mi = v.tensor_copy(idx_res[:, i:i + 1], gidx[:, 0:1])
                        if t == 3:
                            mi.then_inc(a_evg, 1)

                for t in range(4):
                    v_stat1(0, t)
                v_pt1(0)
                for w in range(NW):
                    if w + 1 < NW:
                        v_stat1(w + 1, 0)
                        v_stat1(w + 1, 1)
                        v_kev(w, 0); v_kev(w, 1)
                        v_kev(w, 2); v_kev(w, 3)
                        v_stat1(w + 1, 2)
                        v_kev(w, 4); v_kev(w, 5)
                        v_kev(w, 6); v_kev(w, 7)
                        v_stat1(w + 1, 3)
                    else:
                        for ec in range(NDC):
                            v_kev(w, ec)
                    v_post(w)
                    if w + 1 < NW:
                        v_pt1(w + 1)

            @blk.scalar
            def _(s):
                s.wait_ge(a_pre, 176)

                def s_ln(w, t):
                    par = w % 2
                    i = 4 * w + t
                    xt = xwin[:, par * 4 * D + t * D:par * 4 * D + (t + 1) * D]
                    ss = stats[:, t * 8:t * 8 + 8]
                    s.wait_ge(a_vs, i + 1)
                    if i >= 4:
                        s.wait_ge(a_xn, i - 3)
                    s.activation(sqdump[:], xt, Act.Square, bias=ss[:, 1:2],
                                 scale=cc(C_ONE), accum_out=ss[:, 2:3])
                    s.drain()
                    s.activation(ss[:, 3:4], ss[:, 2:3], Act.Ln,
                                 bias=cc(C_EPS), scale=cc(C_INV1024))
                    s.drain()
                    s.activation(ss[:, 4:5], ss[:, 3:4], Act.Exp,
                                 bias=cc(C_ZERO), scale=cc(C_NEG05)).then_inc(a_sx, 1)

                def s_qev(w, ec):
                    s.wait_ge(a_mmq, 8 * w + ec + 1)
                    if ec == 0 and w >= 1:
                        s.wait_ge(st_qt, 128 * w)
                    s.activation(qt_sb[:, ec * 512:(ec + 1) * 512],
                                 pQK[ec % 2][:, 0:512], Act.Identity,
                                 bias=bqc_s[:, ec:ec + 1],
                                 scale=cc(C_ONE)).then_inc(a_evq, 1)

                def s_xnb(w, t):
                    par = w % 2
                    i = 4 * w + t
                    s.wait_ge(a_xn, i + 1)
                    if w >= 1 and t == 0:
                        s.wait_ge(st_xnb, 64 * w)
                    s.activation(xnb_sb[:, t * D:(t + 1) * D],
                                 xnwin[:, par * 4 * D + t * D:
                                       par * 4 * D + (t + 1) * D],
                                 Act.Identity, bias=cc(C_ZERO),
                                 scale=cc(C_ONE)).then_inc(a_xnb, 1)

                def s_xnt(w, t):
                    par = w % 2
                    i = 4 * w + t
                    s.wait_ge(a_tp, 2 * i + 1)
                    if w >= 2:
                        s.wait_ge(a_mmg, w - 1)
                        s.wait_ge(st_xnt[w % 2], 128 * (w // 2))
                    s.activation(xnTb_v[:, par, 0:4, t, :], pT0_v, Act.Identity,
                                 bias=cc(C_ZERO), scale=cc(C_ONE))
                    s.activation(xnTf_v[:, par, 0:4, t, :], pT0_v, Act.Identity,
                                 bias=cc(C_ZERO),
                                 scale=cc(C_ONE)).then_inc(a_evs, 1)

                for t in range(4):
                    s_ln(0, t)
                for t in range(4):
                    s_xnb(0, t)
                for t in range(4):
                    s_xnt(0, t)
                for w in range(NW):
                    if w + 1 < NW:
                        s_qev(w, 0); s_qev(w, 1); s_qev(w, 2); s_qev(w, 3)
                        s_ln(w + 1, 0)
                        s_qev(w, 4); s_qev(w, 5); s_qev(w, 6); s_qev(w, 7)
                        s_ln(w + 1, 1)
                        s_ln(w + 1, 2)
                        s_ln(w + 1, 3)
                        for t in range(4):
                            s_xnb(w + 1, t)
                        for t in range(4):
                            s_xnt(w + 1, t)
                    else:
                        for ec in range(NDC):
                            s_qev(w, ec)

            @blk.tensor
            def _(t_):
                t_.wait_ge(a_pre, 176)
                t_.wait_ge(a_w, 16 * 3 * NDC)

                def tp(w, t):
                    par = w % 2
                    i = 4 * w + t
                    t_.wait_ge(a_xn, i + 1)
                    if i >= 1:
                        t_.wait_ge(a_evs, i)
                        t_.wait_ge(a_evv, i)
                    xn = xnwin[:, par * 4 * D + t * D:par * 4 * D + (t + 1) * D]
                    for dc in range(8):
                        tr = t_.transpose(pT[dc // 4][:, (dc % 4) * 128:
                                                      (dc % 4 + 1) * 128],
                                          xn[:, dc * 128:(dc + 1) * 128],
                                          ident[:])
                        if dc % 4 == 3:
                            tr.then_inc(a_tp, 1)

                for w in range(NW):
                    par = w % 2
                    for t in range(4):
                        tp(w, t)
                    for ec in range(NDC):
                        if ec < 2:
                            if w >= 1:
                                t_.wait_ge(a_evg, 2 * w)
                            if ec == 0:
                                t_.wait_ge(a_evs, 4 * w + 4)
                                t_.wait_ge(a_evv, 4 * w + 4)
                        else:
                            t_.wait_ge(a_evq, 8 * w + ec - 1)
                        for dc in range(NDC):
                            mm = t_.matmul(pQK[ec % 2][:, 0:512],
                                           wq_b[:, dc * D + ec * 128:
                                                dc * D + (ec + 1) * 128],
                                           xnT_b[:, par * 4096 + dc * 512:
                                                 par * 4096 + (dc + 1) * 512],
                                           start=(dc == 0), stop=(dc == NDC - 1))
                        mm.then_inc(a_mmq, 1)
                    for ec in range(NDC):
                        if ec < 2:
                            t_.wait_ge(a_evq, 8 * w + ec + 7)
                        else:
                            t_.wait_ge(a_evk, 8 * w + ec - 1)
                        for dc in range(NDC):
                            mm = t_.matmul(pQK[ec % 2][:, 0:512],
                                           wk_b[:, dc * D + ec * 128:
                                                dc * D + (ec + 1) * 128],
                                           xnT_b[:, par * 4096 + dc * 512:
                                                 par * 4096 + (dc + 1) * 512],
                                           start=(dc == 0), stop=(dc == NDC - 1))
                        mm.then_inc(a_mmk, 1)
                    for t in range(4):
                        i = 4 * w + t
                        if i >= 2:
                            t_.wait_ge(a_evV, i - 1)
                        for dc in range(NDC):
                            for h in range(2):
                                mm = t_.matmul(pV[2 * (t % 2) + h][:, 0:512],
                                               xnT_b[:, par * 4096 + dc * 512 + t * 128:
                                                     par * 4096 + dc * 512 + (t + 1) * 128],
                                               wv_b[:, dc * D + h * 512:
                                                    dc * D + (h + 1) * 512],
                                               start=(dc == 0), stop=(dc == NDC - 1))
                        mm.then_inc(a_mmv, 1)
                    t_.wait_ge(a_evk, 8 * w + 7)
                    for dc in range(NDC):
                        mm = t_.matmul(pQK[0][0:8, 0:512],
                                       gw_s[:, dc * 8:(dc + 1) * 8],
                                       xnT_f[:, par * 4096 + dc * 512:
                                             par * 4096 + (dc + 1) * 512],
                                       start=(dc == 0), stop=(dc == NDC - 1))
                    mm.then_inc(a_mmg, 1)
                    t_.wait_ge(a_evk, 8 * w + 8)
                    t_.wait_ge(a_evg, 2 * w + 1)
                    for t in range(4):
                        tr = t_.transpose(pQK[1][:, t * 8:(t + 1) * 8],
                                          gT_sb[0:8, t * 128:(t + 1) * 128],
                                          ident[0:8, 0:8])
                        if t == 3:
                            tr.then_inc(a_gt, 1)

        with contextlib.ExitStack() as _ps:
            KT_res = _ps.enter_context(nc.sbuf_tensor("KT_res", [128, NDC * S], BF16))
            V_res = _ps.enter_context(nc.sbuf_tensor("V_res", [128, NT * D], BF16))
            qt_buf = _ps.enter_context(nc.sbuf_tensor("qt_buf", [128, 2 * NDC * 256], BF16))
            pt_buf = _ps.enter_context(nc.sbuf_tensor("pt_buf", [128, 4 * 256], BF16))
            xpn_buf = _ps.enter_context(nc.sbuf_tensor("xpn_buf", [128, 2 * 2048], F32))
            xw_buf = _ps.enter_context(nc.sbuf_tensor("xw_buf", [128, 2 * 2048], F32))
            linv = _ps.enter_context(nc.sbuf_tensor("linv", [128, 2], F32))
            pS0 = _ps.enter_context(nc.psum_tensor("pS0", [128, 512], F32))
            pS1 = _ps.enter_context(nc.psum_tensor("pS1", [128, 512], F32))
            pO0 = _ps.enter_context(nc.psum_tensor("pO0", [128, 512], F32))
            pO1 = _ps.enter_context(nc.psum_tensor("pO1", [128, 512], F32))
            pO2 = _ps.enter_context(nc.psum_tensor("pO2", [128, 512], F32))
            pO3 = _ps.enter_context(nc.psum_tensor("pO3", [128, 512], F32))
            pL0 = _ps.enter_context(nc.psum_tensor("pL0", [128, 512], F32))
            pL1 = _ps.enter_context(nc.psum_tensor("pL1", [128, 512], F32))
            blk = _ps.enter_context(nc.Block("phB"))
            pS = [pS0, pS1]
            pO = [pO0, pO1, pO2, pO3]
            pL = [pL0, pL1]

            @blk.sync
            def _(sy):
                sy.wait_ge(st_kt, 128 * NW)
                for dc in range(NDC // 2):
                    sy.dma_start(KT_res[:, dc * S:(dc + 1) * S],
                                 s_KT[dc, :, :]).then_inc(b_ldk, 16)
                for p in range(2):
                    npar = (NW + 1 - p) // 2
                    if npar:
                        sy.wait_ge(st_xpn[p], 64 * npar)
                        sy.wait_ge(st_xnt[p], 128 * npar)
                sy.wait_ge(st_qt, 128 * NW)
                sy.wait_ge(st_xnb, 64 * NW)
                for s in range(min(2, NQS)):
                    for dc in range(NDC):
                        sy.dma_start(qt_buf[:, (s % 2) * 2048 + dc * 256:
                                            (s % 2) * 2048 + (dc + 1) * 256],
                                     s_QT[dc, :, s * 256:(s + 1) * 256]
                                     ).then_inc(b_qt2[s % 2], 16)
                    for q in range(2):
                        i = 2 * s + q
                        sy.dma_start(xpn_buf[:, (s % 2) * 2048 + q * D:
                                             (s % 2) * 2048 + (q + 1) * D],
                                     s_xpn[i * 128:(i + 1) * 128, :]
                                     ).then_inc(b_xp2[s % 2], 16)
                        sy.dma_start(xw_buf[:, (s % 2) * 2048 + q * D:
                                            (s % 2) * 2048 + (q + 1) * D],
                                     x_h[i * 128:(i + 1) * 128, :]
                                     ).then_inc(b_x2[s % 2], 16)
                for s in range(NQS):
                    sy.wait_ge(b_vo2, s + 1)
                    for q in range(2):
                        i = 2 * s + q
                        sy.dma_start(s_pB[i * 128:(i + 1) * 128, :],
                                     xpn_buf[:, (s % 2) * 2048 + q * D:
                                             (s % 2) * 2048 + (q + 1) * D]
                                     ).then_inc(b_st, 16)
                    if s + 2 < NQS:
                        s2 = s + 2
                        sy.wait_ge(b_sht, NKT * (s + 1))
                        for dc in range(NDC):
                            sy.dma_start(qt_buf[:, (s2 % 2) * 2048 + dc * 256:
                                                (s2 % 2) * 2048 + (dc + 1) * 256],
                                         s_QT[dc, :, s2 * 256:(s2 + 1) * 256]
                                         ).then_inc(b_qt2[s2 % 2], 16)
                        sy.wait_ge(b_st, 32 * (s + 1))
                        for q in range(2):
                            i = 2 * s2 + q
                            sy.dma_start(xpn_buf[:, (s2 % 2) * 2048 + q * D:
                                                 (s2 % 2) * 2048 + (q + 1) * D],
                                         s_xpn[i * 128:(i + 1) * 128, :]
                                         ).then_inc(b_xp2[s2 % 2], 16)
                            sy.dma_start(xw_buf[:, (s2 % 2) * 2048 + q * D:
                                                (s2 % 2) * 2048 + (q + 1) * D],
                                         x_h[i * 128:(i + 1) * 128, :]
                                         ).then_inc(b_x2[s2 % 2], 16)

            @blk.gpsimd
            def _(g):
                g.wait_ge(st_kt, 128 * NW)
                for dc in range(NDC // 2, NDC):
                    g.dma_start(KT_res[:, dc * S:(dc + 1) * S],
                                s_KT[dc, :, :]).then_inc(b_ldk2, 16)
                for p in range(2):
                    npar = (NW + 1 - p) // 2
                    if npar:
                        g.wait_ge(st_xpn[p], 64 * npar)
                        g.wait_ge(st_xnt[p], 128 * npar)
                g.wait_ge(st_qt, 128 * NW)
                g.wait_ge(st_xnb, 64 * NW)
                g.wait_ge(st_v, 16 * NW)
                g.dma_start(V_res[:], s_V[:]).then_inc(b_ldv, 16)

            @blk.tensor
            def _(t_):
                t_.wait_ge(b_ldk, 16 * (NDC // 2))
                t_.wait_ge(b_ldk2, 16 * (NDC - NDC // 2))
                for s in range(NQS):
                    par = s % 2
                    t_.wait_ge(b_qt2[s % 2], 16 * NDC * (s // 2 + 1))
                    if s == 0:
                        t_.wait_ge(b_ldv, 16)
                    LAG = 3
                    for k in range(NKT + LAG):
                        if k < NKT:
                            K = NKT * s + k
                            if K >= 2:
                                t_.wait_ge(b_exp, K - 1)
                            for ec in range(NDC):
                                mm = t_.matmul(pS[k % 2][:, 0:256],
                                               KT_res[:, ec * S + k * 128:
                                                      ec * S + (k + 1) * 128],
                                               qt_buf[:, par * 2048 + ec * 256:
                                                      par * 2048 + (ec + 1) * 256],
                                               start=(ec == 0), stop=(ec == NDC - 1))
                            mm.then_inc(b_sht, 1)
                        j = k - LAG
                        if j >= 0:
                            KJ = NKT * s + j
                            if j == 0 and s >= 1:
                                t_.wait_ge(b_vo, s)
                            t_.wait_ge(b_exp, KJ + 1)
                            for q in range(2):
                                for h in range(2):
                                    t_.matmul(pO[2 * q + h][:, 0:512],
                                              pt_buf[:, (KJ % 4) * 256 + q * 128:
                                                     (KJ % 4) * 256 + (q + 1) * 128],
                                              V_res[:, j * D + h * 512:
                                                    j * D + (h + 1) * 512],
                                              start=(j == 0), stop=(j == NKT - 1))
                                mm = t_.matmul(pL[q][:, 0:1],
                                               pt_buf[:, (KJ % 4) * 256 + q * 128:
                                                      (KJ % 4) * 256 + (q + 1) * 128],
                                               ones_s[:, 0:1],
                                               start=(j == 0), stop=(j == NKT - 1))
                            mm.then_inc(b_pv, 1)

            @blk.scalar
            def _(s_):
                for s in range(NQS):
                    for k in range(NKT):
                        K = NKT * s + k
                        s_.wait_ge(b_sht, K + 1)
                        if K >= 4:
                            s_.wait_ge(b_pv, K - 3)
                        s_.activation(pt_buf[:, (K % 4) * 256:(K % 4) * 256 + 256],
                                      pS[k % 2][:, 0:256], Act.Exp,
                                      bias=cc(C_NEG8),
                                      scale=cc(C_1_32)).then_inc(b_exp, 1)

            @blk.vector
            def _(v):
                for s in range(NQS):
                    par = s % 2
                    v.wait_ge(b_pv, NKT * (s + 1))
                    v.wait_ge(b_xp2[s % 2], 32 * (s // 2 + 1))
                    v.drain()
                    v.reciprocal(linv[:, 0:1], pL[0][:, 0:1])
                    v.reciprocal(linv[:, 1:2], pL[1][:, 0:1])
                    v.drain()
                    for q in range(2):
                        for h in range(2):
                            dst = xpn_buf[:, par * 2048 + q * D + h * 512:
                                          par * 2048 + q * D + (h + 1) * 512]
                            stt = v.scalar_tensor_tensor(
                                dst, pO[2 * q + h][:, 0:512], linv[:, q:q + 1],
                                dst, op0=Alu.mult, op1=Alu.add)
                    stt.then_inc(b_vo, 1)
                    v.drain()
                    v.wait_ge(b_x2[s % 2], 32 * (s // 2 + 1))
                    for q in range(2):
                        dst = xpn_buf[:, par * 2048 + q * D:par * 2048 + (q + 1) * D]
                        tt = v.tensor_tensor(dst, dst,
                                             xw_buf[:, par * 2048 + q * D:
                                                    par * 2048 + (q + 1) * D],
                                             op=Alu.add)
                    tt.then_inc(b_vo2, 1)

        with contextlib.ExitStack() as _ps:
            slot_pi = _ps.enter_context(nc.sbuf_tensor("slot_pi", [128, NT], F32))
            slot_u = _ps.enter_context(nc.sbuf_tensor("slot_u", [128, NT], mybir.dt.uint32))
            perm_pi = _ps.enter_context(nc.sbuf_tensor("perm_pi", [128, NCT], F32))
            perm_u = _ps.enter_context(nc.sbuf_tensor("perm_u", [128, NCT], mybir.dt.uint32))
            wE = _ps.enter_context(nc.sbuf_tensor("wE", [128, 2 * NDC * D], BF16))
            xg = _ps.enter_context(nc.sbuf_tensor("xg", [128, 4 * D], BF16))
            xgT = _ps.enter_context(nc.sbuf_tensor("xgT", [128, 2 * D], BF16))
            y_sb = _ps.enter_context(nc.sbuf_tensor("y_sb", [128, 2 * D], F32))
            _setup_es = contextlib.ExitStack()
            ch8 = _setup_es.enter_context(nc.sbuf_tensor("ch8", [8, S], F32))
            mask8 = _setup_es.enter_context(nc.sbuf_tensor("mask8", [8, S], F32))
            incl8 = _setup_es.enter_context(nc.sbuf_tensor("incl8", [8, S], F32))
            rank_sb = _setup_es.enter_context(nc.sbuf_tensor("rank_sb", [1, S], F32))
            slot_row = _setup_es.enter_context(nc.sbuf_tensor("slot_row", [1, S], F32))
            pfb = _setup_es.enter_context(nc.sbuf_tensor("pfb", [CAPTOT // 128, 128], F32))
            tr1_sb = _setup_es.enter_context(nc.sbuf_tensor("tr1_sb", [NT, 128], F32))
            tr2_sb = _setup_es.enter_context(nc.sbuf_tensor("tr2_sb", [NT, 128], F32))
            pp_sb = _setup_es.enter_context(nc.sbuf_tensor("pp_sb", [CAPTOT // 128, 128], F32))
            _setup_es.close()
            pg = _ps.enter_context(nc.sbuf_tensor("pg", [128, 8 * D], F32))
            cmask = _ps.enter_context(nc.sbuf_tensor("cmask", [128, 1], F32))
            pCt0 = _ps.enter_context(nc.psum_tensor("pCt0", [128, 1024], BF16))
            pCt1 = _ps.enter_context(nc.psum_tensor("pCt1", [128, 1024], BF16))
            pC4 = _ps.enter_context(nc.psum_tensor("pC4", [128, 512], F32))
            pC5 = _ps.enter_context(nc.psum_tensor("pC5", [128, 512], F32))
            pC6 = _ps.enter_context(nc.psum_tensor("pC6", [128, 512], F32))
            pC7 = _ps.enter_context(nc.psum_tensor("pC7", [128, 512], F32))
            blk = _ps.enter_context(nc.Block("phC"))
            pCt = [pCt0, pCt1]
            pCy = [pC4, pC5, pC6, pC7]
            pR = [pC4, pC5]

            NRK = S // 512

            @blk.sync
            def _(sy):
                sy.wait_ge(b_st, 32 * NQS)
                sy.wait_ge(c_e1, 1)
                sy.dma_start(s_m1[:, :], tr1_sb[:]).then_inc(c_s1, 16)
                sy.wait_ge(c_s1, 16)
                m1_t = bass.AP(s_m1.tensor, 0, [[S, 1], [1, S]])
                for r in range(8):
                    sy.dma_start(ch8[r:r + 1, :], m1_t).then_inc(c_s2, 16)
                sy.wait_ge(c_v2, 1)
                sy.dma_start(s_m2[0:1, :], slot_row[0:1, :]).then_inc(c_s3, 16)
                sy.wait_ge(c_s3, 16)
                m2v = bass.AP(s_m2.tensor, 0, [[128, NT], [1, 128]])
                sy.dma_start(tr2_sb[:], m2v).then_inc(c_s4, 16)
                sy.wait_ge(c_gs, 16 * NT)
                sy.dma_start(pp_sb[:], s_perm).then_inc(c_s5, 16)

            @blk.gpsimd
            def _(g_):
                g_.wait_ge(b_st, 32 * NQS)
                g_.dma_start(pfb[:], dumpf_h).then_inc(c_gp, 16)
                g_.wait_ge(c_gp, 16)
                g_.dma_start(s_perm, pfb[:]).then_inc(c_gp, 16)
                g_.wait_ge(c_gp, 32)
                g_.wait_ge(c_v3, 1)
                pflat = bass.AP(s_perm.tensor, 0, [[CAPTOT, 1], [1, CAPTOT]])
                for i in range(NT):
                    g_.indirect_dma_start(
                        pflat, bass.IndirectOffsetOnAxis(slot_u[:, i:i + 1], 1),
                        iotat_s[:, i:i + 1], None).then_inc(c_gs, 16)
                def exp_load(e):
                    if e >= 2:
                        g_.wait_ge(c_my, CAPT * (e - 1))
                    for dc in range(NDC):
                        g_.dma_start(wE[:, (e % 2) * NDC * D + dc * D:
                                        (e % 2) * NDC * D + (dc + 1) * D],
                                     ewT_h[e, dc * 128:(dc + 1) * 128, :]
                                     ).then_inc(c_wl[e % 2], 16)

                exp_load(0)
                exp_load(1)
                g_.wait_ge(st_xnb, 64 * NW)
                g_.wait_ge(c_v4, 1)
                g_.wait_ge(c_z, 16)
                _ins = {CAPT * (e - 1) + max(1, CAPT // 2): e
                        for e in range(2, E)}
                bnd_r = g_.alloc_register(name="bnd")
                g_.reg_mov(bnd_r, S - 1)

                def out_scat(ct):
                    g_.wait_ge(c_oa, ct + 1)
                    g_.indirect_dma_start(
                        out_h, bass.IndirectOffsetOnAxis(perm_u[:, ct:ct + 1], 0),
                        pg[:, (ct % 8) * D:(ct % 8 + 1) * D], None,
                        bounds_check=bnd_r, oob_is_err=False
                    ).then_inc(c_os[ct % 8], 16)

                for ct in range(NCT):
                    if ct in _ins:
                        exp_load(_ins[ct])
                    if ct >= 4:
                        g_.wait_ge(c_tp2, ct - 3)
                    g_.indirect_dma_start(
                        xg[:, (ct % 4) * D:(ct % 4 + 1) * D], None,
                        s_xnb, bass.IndirectOffsetOnAxis(perm_u[:, ct:ct + 1], 0)
                    ).then_inc(c_g2[ct % 4], 16)
                    if ct >= 8:
                        g_.wait_ge(c_os[ct % 8], 16 * ((ct - 8) // 8 + 1))
                    g_.indirect_dma_start(
                        pg[:, (ct % 8) * D:(ct % 8 + 1) * D], None,
                        s_pB, bass.IndirectOffsetOnAxis(perm_u[:, ct:ct + 1], 0)
                    ).then_inc(c_pg[ct % 8], 16)
                    if ct >= 6:
                        out_scat(ct - 6)
                for j in range(NCT - 6, NCT):
                    out_scat(j)

            @blk.vector
            def _(v):
                v.wait_ge(c_t1, 1)
                v.tensor_copy(tr1_sb[:], pR[0][0:NT, 0:128]).then_inc(c_e1, 1)
                v.wait_ge(c_s2, 128)
                v.tensor_scalar(mask8[:], ch8[:], iotac_s[0:8, 0:1], None,
                                op0=Alu.is_equal)
                v.drain()
                v.tensor_tensor_scan(incl8[:], mask8[:], mask8[:], 0.0,
                                     op0=Alu.add, op1=Alu.bypass)
                v.drain()
                v.tensor_tensor(mask8[:], mask8[:], incl8[:],
                                op=Alu.mult).then_inc(c_v1, 1)
                for j in range(NRK):
                    v.wait_ge(c_mr, j + 1)
                    cp = v.tensor_copy(rank_sb[0:1, j * 512:(j + 1) * 512],
                                       pR[j % 2][0:1, 0:512])
                    cp.then_inc(c_rv, 1)
                v.drain()
                v.tensor_scalar(slot_row[0:1, :], ch8[0:1, :], float(CAP), -1.0,
                                op0=Alu.mult, op1=Alu.add)
                v.drain()
                v.tensor_tensor(slot_row[0:1, :], slot_row[0:1, :], rank_sb[0:1, :],
                                op=Alu.add)
                v.drain()
                v.tensor_scalar(slot_row[0:1, :], slot_row[0:1, :],
                                float(CAPTOT - 1), None,
                                op0=Alu.min).then_inc(c_v2, 1)
                v.wait_ge(c_t2, 1)
                v.tensor_copy(slot_pi[:], pR[0][:, 0:NT])
                v.drain()
                v.tensor_copy(slot_u[:], slot_pi[:]).then_inc(c_v3, 1)
                v.wait_ge(c_t3, 1)
                v.tensor_copy(perm_pi[:], pR[1][:, 0:NCT])
                v.drain()
                v.tensor_copy(perm_u[:], perm_pi[:]).then_inc(c_v4, 1)
                for ct in range(NCT):
                    v.wait_ge(c_ye, ct + 1)
                    v.wait_ge(c_pg[ct % 8], 16 * (ct // 8 + 1))
                    v.drain()
                    dst = pg[:, (ct % 8) * D:(ct % 8 + 1) * D]
                    v.tensor_tensor(dst, dst,
                                    y_sb[:, (ct % 2) * D:(ct % 2 + 1) * D],
                                    op=Alu.add).then_inc(c_oa, 1)

            @blk.scalar
            def _(s_):
                def s_xgt(ct):
                    s_.wait_ge(c_tp2, ct + 1)
                    if ct >= 2:
                        s_.wait_ge(c_my, ct - 1)
                    aa = s_.activation(
                        xgT[:, (ct % 2) * D:(ct % 2 + 1) * D],
                        pCt[ct % 2][:, :], Act.Identity,
                        bias=cc(C_ZERO), scale=cc(C_ONE))
                    aa.then_inc(c_xe, 1)

                s_xgt(0)
                for ct in range(NCT):
                    if ct + 1 < NCT:
                        s_xgt(ct + 1)
                    s_.wait_ge(c_my, ct + 1)
                    if ct >= 2:
                        s_.wait_ge(c_oa, ct - 1)
                    for hh in range(2):
                        aa = s_.activation(
                            y_sb[:, (ct % 2) * D + hh * 512:(ct % 2) * D + (hh + 1) * 512],
                            pCy[2 * (ct % 2) + hh][:, 0:512], Act.Identity,
                            bias=cc(C_ZERO), scale=cc(C_ONE))
                    aa.then_inc(c_ye, 1)
                for p in range(8):
                    n = (NCT - p + 7) // 8
                    if n:
                        s_.wait_ge(c_os[p], 16 * n)

            @blk.tensor
            def _(t_):
                t_.transpose(pR[0][0:NT, 0:128], idx_res[:], ident[:]).then_inc(c_t1, 1)
                t_.wait_ge(c_v1, 1)
                for j in range(NRK):
                    if j == 0:
                        t_.wait_ge(c_e1, 1)
                    if j >= 2:
                        t_.wait_ge(c_rv, j - 1)
                    t_.matmul(pR[j % 2][0:1, 0:512], consts[0:8, C_ONE:C_ONE + 1],
                              mask8[:, j * 512:(j + 1) * 512],
                              start=True, stop=True).then_inc(c_mr, 1)
                t_.wait_ge(c_rv, NRK)
                t_.wait_ge(c_s4, 16)
                t_.transpose(pR[0][:, 0:NT], tr2_sb[:],
                             ident[0:NT, 0:NT]).then_inc(c_t2, 1)
                t_.wait_ge(c_s5, 16)
                t_.transpose(pR[1][:, 0:NCT], pp_sb[:],
                             ident[0:CAPTOT // 128, 0:CAPTOT // 128]).then_inc(c_t3, 1)
                t_.wait_ge(c_v3, 1)
                t_.wait_ge(c_v4, 1)

                def tp2(ct):
                    t_.wait_ge(c_g2[ct % 4], 16 * (ct // 4 + 1))
                    if ct >= 2:
                        t_.wait_ge(c_xe, ct - 1)
                    for dc in range(8):
                        tr = t_.transpose(
                            pCt[ct % 2][:, dc * 128:(dc + 1) * 128],
                            xg[:, (ct % 4) * D + dc * 128:(ct % 4) * D + (dc + 1) * 128],
                            identb[:])
                        if dc == 7:
                            tr.then_inc(c_tp2, 1)

                tp2(0)
                tp2(1)
                for ct in range(NCT):
                    e = ct // CAPT
                    t_.wait_ge(c_wl[e % 2], 16 * NDC * (e // 2 + 1))
                    t_.wait_ge(c_xe, ct + 1)
                    if ct >= 2:
                        t_.wait_ge(c_ye, ct - 1)
                    for dc in range(NDC):
                        for hh in range(2):
                            mm = t_.matmul(
                                pCy[2 * (ct % 2) + hh][:, 0:512],
                                xgT[:, (ct % 2) * D + dc * 128:(ct % 2) * D + (dc + 1) * 128],
                                wE[:, (e % 2) * NDC * D + dc * D + hh * 512:
                                   (e % 2) * NDC * D + dc * D + (hh + 1) * 512],
                                start=(dc == 0), stop=(dc == NDC - 1))
                    mm.then_inc(c_my, 1)
                    if ct + 2 < NCT:
                        tp2(ct + 2)

    return nc


def _host_inputs(inputs, S=4096):
    f32 = np.float32
    Wq = np.asarray(inputs["L0_attn_Wq"], f32)
    Wk = np.asarray(inputs["L0_attn_Wk"], f32)
    Wv = np.asarray(inputs["L0_attn_Wv"], f32)
    bq = np.asarray(inputs["L0_attn_bq"], f32)
    bk = np.asarray(inputs["L0_attn_bk"], f32)
    bv = np.asarray(inputs["L0_attn_bv"], f32)
    gw = np.asarray(inputs["L0_ffn_gate_w"], f32)
    ew = np.asarray(inputs["L0_ffn_expert_w"], f32)
    shared = {
        "wqT": np.ascontiguousarray(Wq.T),
        "wkT": np.ascontiguousarray(Wk.T),
        "wvT": np.ascontiguousarray(Wv.T),
        "ewT": np.ascontiguousarray(ew.transpose(0, 2, 1)),
        "gwT": np.ascontiguousarray(gw.reshape(8, 8, 128).transpose(2, 1, 0)
                                    .reshape(128, 64)),
        "bqc": np.ascontiguousarray(bq.reshape(8, 128).T),
        "bkc": np.ascontiguousarray(bk.reshape(8, 128).T),
        "bvb": np.ascontiguousarray(np.broadcast_to(bv, (128, D))),
        "cst": np.ascontiguousarray(np.broadcast_to(
            np.array(CONST_VALS + [0.0], f32), (128, 8))),
        "idm": np.eye(128, dtype=f32),
        "idmb": np.eye(128, dtype=ml_dtypes.bfloat16),
        "onesb": np.ones((128, 1), dtype=ml_dtypes.bfloat16),
        "iotac": np.arange(128, dtype=f32).reshape(128, 1),
        "iotat": np.ascontiguousarray(
            (np.arange(S // 128, dtype=f32)[None, :] * 128
             + np.arange(128, dtype=f32)[:, None])),
        "dumpf": np.full((2 * S // 128, 128), float(S), dtype=f32),
        "zrow": np.zeros((1, D), dtype=ml_dtypes.bfloat16),
        "zrowf": np.zeros((1, D), dtype=np.float32),
    }
    return shared


_NC_CACHE = {}


def kernel(L0_attn_x, L0_attn_Wq, L0_attn_bq, L0_attn_Wk, L0_attn_bk,
           L0_attn_Wv, L0_attn_bv, L0_ffn_gate_w, L0_ffn_expert_w):
    S = 4096
    inputs = dict(L0_attn_x=L0_attn_x, L0_attn_Wq=L0_attn_Wq, L0_attn_bq=L0_attn_bq,
                  L0_attn_Wk=L0_attn_Wk, L0_attn_bk=L0_attn_bk, L0_attn_Wv=L0_attn_Wv,
                  L0_attn_bv=L0_attn_bv, L0_ffn_gate_w=L0_ffn_gate_w,
                  L0_ffn_expert_w=L0_ffn_expert_w)
    shared = _host_inputs(inputs, S)
    x = np.asarray(L0_attn_x, np.float32)
    in_maps = []
    for b in range(8):
        m = dict(shared)
        m["x"] = np.ascontiguousarray(x[b])
        in_maps.append(m)
    if S not in _NC_CACHE:
        _NC_CACHE[S] = build_nc(S)
    nc = _NC_CACHE[S]
    res = run_bass_kernel_spmd(nc, in_maps, core_ids=list(range(8)))
    out = np.stack([np.asarray(r["out"], np.float32)[:S] for r in res.results])
    return out


# version 30
# speedup vs baseline: 1.1262x; 1.0365x over previous
import numpy as np
import ml_dtypes
import concourse.bass as bass
import concourse.mybir as mybir
from concourse.bass_utils import run_bass_kernel_spmd

F32 = mybir.dt.float32
BF16 = mybir.dt.bfloat16
Act = mybir.ActivationFunctionType
Alu = mybir.AluOpType
AX = mybir.AxisListType.X

D = 1024
NDC = 8
E = 8
C_ONE, C_ZERO, C_EPS, C_INV1024, C_NEG05, C_1_32, C_NEG8 = range(7)
CONST_VALS = [1.0, 0.0, 1e-5, 1.0 / 1024.0, -0.5, 1.0 / 32.0, -8.0]


def build_nc(S=4096):
    NT = S // 128
    NT2 = NT // 2
    NW = S // 512
    NQS = S // 256
    NKT = S // 128
    NK = 32 * NQS
    NST = S // 512
    CAPT = 2 * (S // 128) // E
    CAP = CAPT * 128
    CAPTOT = E * CAP
    NCT = E * CAPT

    nc = bass.Bass("TRN2", target_bir_lowering=False)

    x_h = nc.dram_tensor("x", [S, D], F32, kind="ExternalInput").ap()
    wqT_h = nc.dram_tensor("wqT", [D, D], F32, kind="ExternalInput").ap()
    wkT_h = nc.dram_tensor("wkT", [D, D], F32, kind="ExternalInput").ap()
    wvT_h = nc.dram_tensor("wvT", [D, D], F32, kind="ExternalInput").ap()
    ewT_h = nc.dram_tensor("ewT", [E, D, D], F32, kind="ExternalInput").ap()
    gw_h = nc.dram_tensor("gwT", [128, NDC * E], F32, kind="ExternalInput").ap()
    bqc_h = nc.dram_tensor("bqc", [128, 8], F32, kind="ExternalInput").ap()
    bkc_h = nc.dram_tensor("bkc", [128, 8], F32, kind="ExternalInput").ap()
    bvb_h = nc.dram_tensor("bvb", [128, D], F32, kind="ExternalInput").ap()
    cst_h = nc.dram_tensor("cst", [128, 8], F32, kind="ExternalInput").ap()
    idm_h = nc.dram_tensor("idm", [128, 128], F32, kind="ExternalInput").ap()
    idmb_h = nc.dram_tensor("idmb", [128, 128], BF16, kind="ExternalInput").ap()
    ones_h = nc.dram_tensor("onesb", [128, 1], BF16, kind="ExternalInput").ap()
    iotac_h = nc.dram_tensor("iotac", [128, 1], F32, kind="ExternalInput").ap()
    iotat_h = nc.dram_tensor("iotat", [128, NT], F32, kind="ExternalInput").ap()
    dumpf_h = nc.dram_tensor("dumpf", [CAPTOT // 128, 128], F32, kind="ExternalInput").ap()
    zrow_h = nc.dram_tensor("zrow", [1, D], BF16, kind="ExternalInput").ap()
    zrowf_h = nc.dram_tensor("zrowf", [1, D], F32, kind="ExternalInput").ap()
    out_h = nc.dram_tensor("out", [S, D], F32, kind="ExternalOutput").ap()

    s_xnT = nc.dram_tensor("s_xnT", [NDC, 128, S], BF16, kind="Internal").ap()
    s_QT = nc.dram_tensor("s_QT", [NDC, 128, S], BF16, kind="Internal").ap()
    s_KT = nc.dram_tensor("s_KT", [NDC, 128, S], BF16, kind="Internal").ap()
    s_V = nc.dram_tensor("s_V", [128, NT * D], BF16, kind="Internal").ap()
    s_xpn = nc.dram_tensor("s_xpn", [S, D], F32, kind="Internal").ap()
    s_pB = nc.dram_tensor("s_pB", [S + 1, D], F32, kind="Internal").ap()
    s_xnb = nc.dram_tensor("s_xnb", [S + 1, D], BF16, kind="Internal").ap()
    s_perm = nc.dram_tensor("s_perm", [CAPTOT // 128, 128], F32, kind="Internal").ap()
    s_m1 = nc.dram_tensor("s_m1", [NT, 128], F32, kind="Internal").ap()
    s_m2 = nc.dram_tensor("s_m2", [1, S], F32, kind="Internal").ap()

    import contextlib
    with contextlib.ExitStack() as _es:
        consts = _es.enter_context(nc.sbuf_tensor("consts", [128, 8], F32))
        ident = _es.enter_context(nc.sbuf_tensor("ident", [128, 128], F32))
        ones_s = _es.enter_context(nc.sbuf_tensor("ones_s", [128, 1], BF16))
        idx_res = _es.enter_context(nc.sbuf_tensor("idx_res", [128, NT], F32))
        identb = _es.enter_context(nc.sbuf_tensor("identb", [128, 128], BF16))
        iotac_s = _es.enter_context(nc.sbuf_tensor("iotac_s", [128, 1], F32))
        iotat_s = _es.enter_context(nc.sbuf_tensor("iotat_s", [128, NT], F32))
        a_pre = _es.enter_context(nc.semaphore("a_pre"))
        a_w = _es.enter_context(nc.semaphore("a_w"))
        a_x = _es.enter_context(nc.semaphore("a_x"))
        a_vs = _es.enter_context(nc.semaphore("a_vs"))
        a_sx = _es.enter_context(nc.semaphore("a_sx"))
        a_xn = _es.enter_context(nc.semaphore("a_xn"))
        a_xpn = _es.enter_context(nc.semaphore("a_xpn"))
        a_tp = _es.enter_context(nc.semaphore("a_tp"))
        a_evs = _es.enter_context(nc.semaphore("a_evs"))
        a_evv = _es.enter_context(nc.semaphore("a_evv"))
        a_mmq = _es.enter_context(nc.semaphore("a_mmq"))
        a_evq = _es.enter_context(nc.semaphore("a_evq"))
        a_mmk = _es.enter_context(nc.semaphore("a_mmk"))
        a_evk = _es.enter_context(nc.semaphore("a_evk"))
        a_mmv = _es.enter_context(nc.semaphore("a_mmv"))
        a_evV = _es.enter_context(nc.semaphore("a_evV"))
        a_mmg = _es.enter_context(nc.semaphore("a_mmg"))
        a_evg = _es.enter_context(nc.semaphore("a_evg"))
        a_gt = _es.enter_context(nc.semaphore("a_gt"))
        st_xpn = [_es.enter_context(nc.semaphore(f"st_xpn{p}")) for p in range(2)]
        st_xnt = [_es.enter_context(nc.semaphore(f"st_xnt{p}")) for p in range(2)]
        st_qt = _es.enter_context(nc.semaphore("st_qt"))
        st_kt = _es.enter_context(nc.semaphore("st_kt"))
        st_v = _es.enter_context(nc.semaphore("st_v"))
        st_xnb = _es.enter_context(nc.semaphore("st_xnb"))
        a_xnb = _es.enter_context(nc.semaphore("a_xnb"))
        a_xp2 = [_es.enter_context(nc.semaphore(f"a_xp{p}")) for p in range(2)]
        b_ldk = _es.enter_context(nc.semaphore("b_ldk"))
        b_ldk2 = _es.enter_context(nc.semaphore("b_ldk2"))
        b_ldv = _es.enter_context(nc.semaphore("b_ldv"))
        b_qt2 = [_es.enter_context(nc.semaphore(f"b_qt{p}")) for p in range(2)]
        b_xp2 = [_es.enter_context(nc.semaphore(f"b_xp{p}")) for p in range(2)]
        b_x2 = [_es.enter_context(nc.semaphore(f"b_x{p}")) for p in range(2)]
        b_vo2 = _es.enter_context(nc.semaphore("b_vo2"))
        b_sht = _es.enter_context(nc.semaphore("b_sht"))
        b_exp = _es.enter_context(nc.semaphore("b_exp"))
        b_pv = _es.enter_context(nc.semaphore("b_pv"))
        b_vo = _es.enter_context(nc.semaphore("b_vo"))
        b_st = _es.enter_context(nc.semaphore("b_st"))
        c_s1 = _es.enter_context(nc.semaphore("c_s1"))
        c_t1 = _es.enter_context(nc.semaphore("c_t1"))
        c_e1 = _es.enter_context(nc.semaphore("c_e1"))
        c_t2 = _es.enter_context(nc.semaphore("c_t2"))
        c_t3 = _es.enter_context(nc.semaphore("c_t3"))
        c_s2 = _es.enter_context(nc.semaphore("c_s2"))
        c_s3 = _es.enter_context(nc.semaphore("c_s3"))
        c_s4 = _es.enter_context(nc.semaphore("c_s4"))
        c_s5 = _es.enter_context(nc.semaphore("c_s5"))
        c_v1 = _es.enter_context(nc.semaphore("c_v1"))
        c_v2 = _es.enter_context(nc.semaphore("c_v2"))
        c_v3 = _es.enter_context(nc.semaphore("c_v3"))
        c_v4 = _es.enter_context(nc.semaphore("c_v4"))
        c_mr = _es.enter_context(nc.semaphore("c_mr"))
        c_rv = _es.enter_context(nc.semaphore("c_rv"))
        c_gp = _es.enter_context(nc.semaphore("c_gp"))
        c_gs = _es.enter_context(nc.semaphore("c_gs"))
        c_g2 = [_es.enter_context(nc.semaphore(f"c_g{p}")) for p in range(4)]
        c_wl = [_es.enter_context(nc.semaphore(f"c_wl{p}")) for p in range(2)]
        c_tp2 = _es.enter_context(nc.semaphore("c_tp2"))
        c_xe = _es.enter_context(nc.semaphore("c_xe"))
        c_my = _es.enter_context(nc.semaphore("c_my"))
        c_ye = _es.enter_context(nc.semaphore("c_ye"))
        c_pg = [_es.enter_context(nc.semaphore(f"c_pg{p}")) for p in range(8)]
        c_os = [_es.enter_context(nc.semaphore(f"c_os{p}")) for p in range(8)]
        c_oa = _es.enter_context(nc.semaphore("c_oa"))
        c_z = _es.enter_context(nc.semaphore("c_z"))
        for j, v in enumerate(CONST_VALS):
            nc.const_aps.register(F32, v, consts[:, j:j + 1])

        def cc(j):
            return consts[:, j:j + 1]

        with contextlib.ExitStack() as _ps:
            wq_b = _ps.enter_context(nc.sbuf_tensor("wq_b", [128, NDC * D], BF16))
            wk_b = _ps.enter_context(nc.sbuf_tensor("wk_b", [128, NDC * D], BF16))
            wv_b = _ps.enter_context(nc.sbuf_tensor("wv_b", [128, NDC * D], BF16))
            gw_s = _ps.enter_context(nc.sbuf_tensor("gw_s", [128, NDC * E], F32))
            bqc_s = _ps.enter_context(nc.sbuf_tensor("bqc_s", [128, 8], F32))
            bkc_s = _ps.enter_context(nc.sbuf_tensor("bkc_s", [128, 8], F32))
            bvb_s = _ps.enter_context(nc.sbuf_tensor("bvb_s", [128, D], F32))
            xwin = _ps.enter_context(nc.sbuf_tensor("xwin", [128, 2 * 4 * D], F32))
            xnwin = _ps.enter_context(nc.sbuf_tensor("xnwin", [128, 2 * 4 * D], F32))
            sqdump = _ps.enter_context(nc.sbuf_tensor("sqdump", [128, D], F32))
            stats = _ps.enter_context(nc.sbuf_tensor("stats", [128, 4 * 8], F32))
            xnb_sb = _ps.enter_context(nc.sbuf_tensor("xnb_sb", [128, 4 * D], BF16))
            xnT_b = _ps.enter_context(nc.sbuf_tensor("xnT_b", [128, 2 * NDC * 512], BF16))
            xnT_f = _ps.enter_context(nc.sbuf_tensor("xnT_f", [128, 2 * NDC * 512], F32))
            qt_sb = _ps.enter_context(nc.sbuf_tensor("qt_sb", [128, NDC * 512], BF16))
            kt_sb = _ps.enter_context(nc.sbuf_tensor("kt_sb", [128, NDC * 512], BF16))
            vt_sb = _ps.enter_context(nc.sbuf_tensor("vt_sb", [128, 4 * D], BF16))
            gT_sb = _ps.enter_context(nc.sbuf_tensor("gT_sb", [128, 512], F32))
            gtok = _ps.enter_context(nc.sbuf_tensor("gtok", [128, 8], F32))
            gmax = _ps.enter_context(nc.sbuf_tensor("gmax", [128, 8], F32))
            gidx = _ps.enter_context(nc.sbuf_tensor("gidx", [128, 8], mybir.dt.uint32))
            pT0 = _ps.enter_context(nc.psum_tensor("pT0", [128, 512], F32))
            pT1 = _ps.enter_context(nc.psum_tensor("pT1", [128, 512], F32))
            pQK0 = _ps.enter_context(nc.psum_tensor("pQK0", [128, 512], F32))
            pQK1 = _ps.enter_context(nc.psum_tensor("pQK1", [128, 512], F32))
            pV0 = _ps.enter_context(nc.psum_tensor("pV0", [128, 512], F32))
            pV1 = _ps.enter_context(nc.psum_tensor("pV1", [128, 512], F32))
            pV2 = _ps.enter_context(nc.psum_tensor("pV2", [128, 512], F32))
            pV3 = _ps.enter_context(nc.psum_tensor("pV3", [128, 512], F32))
            blk = _ps.enter_context(nc.Block("phA"))
            pT = [pT0, pT1]
            pQK = [pQK0, pQK1]
            pV = [pV0, pV1, pV2, pV3]
            xnTb_v = xnT_b[:].rearrange("p (a c t s) -> p a c t s", a=2, c=NDC, t=4)
            xnTf_v = xnT_f[:].rearrange("p (a c t s) -> p a c t s", a=2, c=NDC, t=4)
            pT0_v = pT0[:].rearrange("p (c s) -> p c s", c=4)
            pT1_v = pT1[:].rearrange("p (c s) -> p c s", c=4)

            @blk.sync
            def _(sy):
                sy.dma_start(consts[:], cst_h).then_inc(a_pre, 16)
                sy.dma_start(ident[:], idm_h).then_inc(a_pre, 16)
                sy.dma_start(ones_s[:], ones_h).then_inc(a_pre, 16)
                sy.dma_start(gw_s[:], gw_h).then_inc(a_pre, 16)
                sy.dma_start(bqc_s[:], bqc_h).then_inc(a_pre, 16)
                sy.dma_start(bkc_s[:], bkc_h).then_inc(a_pre, 16)
                sy.dma_start(bvb_s[:], bvb_h).then_inc(a_pre, 16)
                sy.dma_start(identb[:], idmb_h).then_inc(a_pre, 16)
                sy.dma_start(iotac_s[:], iotac_h).then_inc(a_pre, 16)
                sy.dma_start(iotat_s[:], iotat_h).then_inc(a_pre, 16)
                sy.dma_start(s_xnb[S:S + 1, :], zrow_h).then_inc(a_pre, 16)
                sy.dma_start(s_pB[S:S + 1, :], zrowf_h).then_inc(c_z, 16)
                for w in range(NW):
                    for t in range(4):
                        i = 4 * w + t
                        sy.wait_ge(a_xn, i + 1)
                        sy.dma_start(s_xpn[i * 128:(i + 1) * 128, :],
                                     xnwin[:, (w % 2) * 4 * D + t * D:
                                           (w % 2) * 4 * D + (t + 1) * D]
                                     ).then_inc(st_xpn[w % 2], 16)
                    for t in range(4):
                        i = 4 * w + t
                        sy.wait_ge(a_xnb, i + 1)
                        sy.dma_start(s_xnb[i * 128:(i + 1) * 128, :],
                                     xnb_sb[:, t * D:(t + 1) * D]
                                     ).then_inc(st_xnb, 16)
                    sy.wait_ge(a_evs, 4 * w + 4)
                    sy.wait_ge(a_evv, 4 * w + 4)
                    for dc in range(NDC):
                        sy.dma_start(
                            s_xnT[dc, :, w * 512:(w + 1) * 512],
                            xnT_b[:, (w % 2) * 4096 + dc * 512:
                                  (w % 2) * 4096 + (dc + 1) * 512]
                        ).then_inc(st_xnt[w % 2], 16)
                    for ec in range(NDC):
                        sy.wait_ge(a_evq, 8 * w + ec + 1)
                        sy.dma_start(s_QT[ec, :, w * 512:(w + 1) * 512],
                                     qt_sb[:, ec * 512:(ec + 1) * 512]
                                     ).then_inc(st_qt, 16)
                    for ec in range(NDC):
                        sy.wait_ge(a_evk, 8 * w + ec + 1)
                        sy.dma_start(s_KT[ec, :, w * 512:(w + 1) * 512],
                                     kt_sb[:, ec * 512:(ec + 1) * 512]
                                     ).then_inc(st_kt, 16)
                    sy.wait_ge(a_evV, 4 * w + 4)
                    sy.dma_start(s_V[:, w * 4 * D:(w + 1) * 4 * D],
                                 vt_sb[:]).then_inc(st_v, 16)

            @blk.gpsimd
            def _(g):
                wjobs = [(wq_b, wqT_h), (wk_b, wkT_h), (wv_b, wvT_h)]
                def load_x(w):
                    if w >= 2:
                        g.wait_ge(a_xn, 4 * (w - 1))
                    for t in range(4):
                        i = 4 * w + t
                        g.dma_start(xwin[:, (w % 2) * 4 * D + t * D:
                                         (w % 2) * 4 * D + (t + 1) * D],
                                    x_h[i * 128:(i + 1) * 128, :]).then_inc(a_xp2[w % 2], 16)
                def load_w(j):
                    wsb, wh = wjobs[j]
                    for dc in range(NDC):
                        g.dma_start(wsb[:, dc * D:(dc + 1) * D],
                                    wh[dc * 128:(dc + 1) * 128, :]).then_inc(a_w, 16)
                load_x(0)
                load_w(0)
                if NW > 1:
                    load_x(1)
                load_w(1)
                load_w(2)
                for w in range(2, NW):
                    load_x(w)

            @blk.vector
            def _(v):
                v.wait_ge(a_pre, 176)
                v.memset(gmax[:], 0)
                v.drain()

                def v_stat1(w, t):
                    par = w % 2
                    i = 4 * w + t
                    xt = xwin[:, par * 4 * D + t * D:par * 4 * D + (t + 1) * D]
                    xn = xnwin[:, par * 4 * D + t * D:par * 4 * D + (t + 1) * D]
                    ss = stats[:, t * 8:t * 8 + 8]
                    if t == 0:
                        v.wait_ge(a_xp2[w % 2], 64 * (w // 2 + 1))
                    v.tensor_reduce(ss[:, 0:1], xt, axis=AX, op=Alu.add)
                    v.drain()
                    v.tensor_scalar(ss[:, 1:2], ss[:, 0:1], -1.0 / 1024.0, None,
                                    op0=Alu.mult).then_inc(a_vs, 1)
                    v.wait_ge(a_sx, i + 1)
                    v.tensor_tensor(ss[:, 5:6], ss[:, 1:2], ss[:, 4:5],
                                    op=Alu.mult)
                    v.drain()
                    if i >= 8:
                        v.wait_ge(a_tp, 2 * (i - 8) + 2)
                        v.wait_ge(st_xpn[par], 64 * (w // 2))
                        v.wait_ge(a_xnb, i - 4)
                    v.tensor_scalar(xn, xt, ss[:, 4:5], ss[:, 5:6],
                                    op0=Alu.mult, op1=Alu.add).then_inc(a_xn, 1)

                def v_kev(w, ec):
                    v.wait_ge(a_mmk, 8 * w + ec + 1)
                    if ec == 0 and w >= 1:
                        v.wait_ge(st_kt, 128 * w)
                    v.tensor_scalar(kt_sb[:, ec * 512:(ec + 1) * 512],
                                    pQK[ec % 2][:, 0:512], bkc_s[:, ec:ec + 1],
                                    None, op0=Alu.add).then_inc(a_evk, 1)

                def v_pt1_1(w, t):
                    par = w % 2
                    i = 4 * w + t
                    v.wait_ge(a_tp, 2 * i + 2)
                    if w >= 2:
                        v.wait_ge(a_mmg, w - 1)
                        v.wait_ge(st_xnt[w % 2], 128 * (w // 2))
                    v.tensor_copy(xnTb_v[:, par, 4:8, t, :], pT1_v)
                    v.tensor_copy(xnTf_v[:, par, 4:8, t, :],
                                  pT1_v).then_inc(a_evv, 1)

                def v_pt1(w):
                    for t in range(4):
                        v_pt1_1(w, t)

                def v_post(w):
                    for t in range(4):
                        i = 4 * w + t
                        v.wait_ge(a_mmv, i + 1)
                        if w >= 1 and t == 0:
                            v.wait_ge(st_v, 16 * w)
                        for h in range(2):
                            tt = v.tensor_tensor(
                                vt_sb[:, t * D + h * 512:t * D + (h + 1) * 512],
                                pV[2 * (t % 2) + h][:, 0:512],
                                bvb_s[:, h * 512:(h + 1) * 512], op=Alu.add)
                            if h == 1:
                                tt.then_inc(a_evV, 1)
                    v.wait_ge(a_mmg, w + 1)
                    v.tensor_copy(gT_sb[0:8, :], pQK[0][0:8, :]).then_inc(a_evg, 1)
                    v.wait_ge(a_gt, w + 1)
                    for t in range(4):
                        i = 4 * w + t
                        v.drain()
                        v.tensor_copy(gtok[:], pQK[1][:, t * 8:t * 8 + 8])
                        v.drain()
                        v.tensor_reduce(gmax[:, 0:1], gtok[:], axis=AX, op=Alu.max)
                        v.drain()
                        v.max_index(gidx[:], gmax[:], gtok[:])
                        v.drain()
                        mi = v.tensor_copy(idx_res[:, i:i + 1], gidx[:, 0:1])
                        if t == 3:
                            mi.then_inc(a_evg, 1)

                for t in range(4):
                    v_stat1(0, t)
                v_pt1(0)
                for w in range(NW):
                    if w + 1 < NW:
                        v_stat1(w + 1, 0)
                        v_stat1(w + 1, 1)
                        v_kev(w, 0); v_kev(w, 1)
                        v_kev(w, 2); v_kev(w, 3)
                        v_stat1(w + 1, 2)
                        v_kev(w, 4); v_kev(w, 5)
                        v_kev(w, 6); v_kev(w, 7)
                        v_stat1(w + 1, 3)
                    else:
                        for ec in range(NDC):
                            v_kev(w, ec)
                    v_post(w)
                    if w + 1 < NW:
                        v_pt1(w + 1)

            @blk.scalar
            def _(s):
                s.wait_ge(a_pre, 176)

                def s_ln(w, t):
                    par = w % 2
                    i = 4 * w + t
                    xt = xwin[:, par * 4 * D + t * D:par * 4 * D + (t + 1) * D]
                    ss = stats[:, t * 8:t * 8 + 8]
                    s.wait_ge(a_vs, i + 1)
                    if i >= 4:
                        s.wait_ge(a_xn, i - 3)
                    s.activation(sqdump[:], xt, Act.Square, bias=ss[:, 1:2],
                                 scale=cc(C_ONE), accum_out=ss[:, 2:3])
                    s.drain()
                    s.activation(ss[:, 3:4], ss[:, 2:3], Act.Ln,
                                 bias=cc(C_EPS), scale=cc(C_INV1024))
                    s.drain()
                    s.activation(ss[:, 4:5], ss[:, 3:4], Act.Exp,
                                 bias=cc(C_ZERO), scale=cc(C_NEG05)).then_inc(a_sx, 1)

                def s_qev(w, ec):
                    s.wait_ge(a_mmq, 8 * w + ec + 1)
                    if ec == 0 and w >= 1:
                        s.wait_ge(st_qt, 128 * w)
                    s.activation(qt_sb[:, ec * 512:(ec + 1) * 512],
                                 pQK[ec % 2][:, 0:512], Act.Identity,
                                 bias=bqc_s[:, ec:ec + 1],
                                 scale=cc(C_ONE)).then_inc(a_evq, 1)

                def s_xnb(w, t):
                    par = w % 2
                    i = 4 * w + t
                    s.wait_ge(a_xn, i + 1)
                    if w >= 1 and t == 0:
                        s.wait_ge(st_xnb, 64 * w)
                    s.activation(xnb_sb[:, t * D:(t + 1) * D],
                                 xnwin[:, par * 4 * D + t * D:
                                       par * 4 * D + (t + 1) * D],
                                 Act.Identity, bias=cc(C_ZERO),
                                 scale=cc(C_ONE)).then_inc(a_xnb, 1)

                def s_xnt(w, t):
                    par = w % 2
                    i = 4 * w + t
                    s.wait_ge(a_tp, 2 * i + 1)
                    if w >= 2:
                        s.wait_ge(a_mmg, w - 1)
                        s.wait_ge(st_xnt[w % 2], 128 * (w // 2))
                    s.activation(xnTb_v[:, par, 0:4, t, :], pT0_v, Act.Identity,
                                 bias=cc(C_ZERO), scale=cc(C_ONE))
                    s.activation(xnTf_v[:, par, 0:4, t, :], pT0_v, Act.Identity,
                                 bias=cc(C_ZERO),
                                 scale=cc(C_ONE)).then_inc(a_evs, 1)

                for t in range(4):
                    s_ln(0, t)
                for t in range(4):
                    s_xnb(0, t)
                for t in range(4):
                    s_xnt(0, t)
                for w in range(NW):
                    if w + 1 < NW:
                        s_qev(w, 0); s_qev(w, 1); s_qev(w, 2); s_qev(w, 3)
                        s_ln(w + 1, 0)
                        s_qev(w, 4); s_qev(w, 5); s_qev(w, 6); s_qev(w, 7)
                        s_ln(w + 1, 1)
                        s_ln(w + 1, 2)
                        s_ln(w + 1, 3)
                        for t in range(4):
                            s_xnb(w + 1, t)
                        for t in range(4):
                            s_xnt(w + 1, t)
                    else:
                        for ec in range(NDC):
                            s_qev(w, ec)

            @blk.tensor
            def _(t_):
                t_.wait_ge(a_pre, 176)
                t_.wait_ge(a_w, 16 * 3 * NDC)

                def tp(w, t):
                    par = w % 2
                    i = 4 * w + t
                    t_.wait_ge(a_xn, i + 1)
                    if i >= 1:
                        t_.wait_ge(a_evs, i)
                        t_.wait_ge(a_evv, i)
                    xn = xnwin[:, par * 4 * D + t * D:par * 4 * D + (t + 1) * D]
                    for dc in range(8):
                        tr = t_.transpose(pT[dc // 4][:, (dc % 4) * 128:
                                                      (dc % 4 + 1) * 128],
                                          xn[:, dc * 128:(dc + 1) * 128],
                                          ident[:])
                        if dc % 4 == 3:
                            tr.then_inc(a_tp, 1)

                for w in range(NW):
                    par = w % 2
                    for t in range(4):
                        tp(w, t)
                    for ec in range(NDC):
                        if ec < 2:
                            if w >= 1:
                                t_.wait_ge(a_evg, 2 * w)
                            if ec == 0:
                                t_.wait_ge(a_evs, 4 * w + 4)
                                t_.wait_ge(a_evv, 4 * w + 4)
                        else:
                            t_.wait_ge(a_evq, 8 * w + ec - 1)
                        for dc in range(NDC):
                            mm = t_.matmul(pQK[ec % 2][:, 0:512],
                                           wq_b[:, dc * D + ec * 128:
                                                dc * D + (ec + 1) * 128],
                                           xnT_b[:, par * 4096 + dc * 512:
                                                 par * 4096 + (dc + 1) * 512],
                                           start=(dc == 0), stop=(dc == NDC - 1))
                        mm.then_inc(a_mmq, 1)
                    for ec in range(NDC):
                        if ec < 2:
                            t_.wait_ge(a_evq, 8 * w + ec + 7)
                        else:
                            t_.wait_ge(a_evk, 8 * w + ec - 1)
                        for dc in range(NDC):
                            mm = t_.matmul(pQK[ec % 2][:, 0:512],
                                           wk_b[:, dc * D + ec * 128:
                                                dc * D + (ec + 1) * 128],
                                           xnT_b[:, par * 4096 + dc * 512:
                                                 par * 4096 + (dc + 1) * 512],
                                           start=(dc == 0), stop=(dc == NDC - 1))
                        mm.then_inc(a_mmk, 1)
                    for t in range(4):
                        i = 4 * w + t
                        if i >= 2:
                            t_.wait_ge(a_evV, i - 1)
                        for dc in range(NDC):
                            for h in range(2):
                                mm = t_.matmul(pV[2 * (t % 2) + h][:, 0:512],
                                               xnT_b[:, par * 4096 + dc * 512 + t * 128:
                                                     par * 4096 + dc * 512 + (t + 1) * 128],
                                               wv_b[:, dc * D + h * 512:
                                                    dc * D + (h + 1) * 512],
                                               start=(dc == 0), stop=(dc == NDC - 1))
                        mm.then_inc(a_mmv, 1)
                    t_.wait_ge(a_evk, 8 * w + 7)
                    for dc in range(NDC):
                        mm = t_.matmul(pQK[0][0:8, 0:512],
                                       gw_s[:, dc * 8:(dc + 1) * 8],
                                       xnT_f[:, par * 4096 + dc * 512:
                                             par * 4096 + (dc + 1) * 512],
                                       start=(dc == 0), stop=(dc == NDC - 1))
                    mm.then_inc(a_mmg, 1)
                    t_.wait_ge(a_evk, 8 * w + 8)
                    t_.wait_ge(a_evg, 2 * w + 1)
                    for t in range(4):
                        tr = t_.transpose(pQK[1][:, t * 8:(t + 1) * 8],
                                          gT_sb[0:8, t * 128:(t + 1) * 128],
                                          ident[0:8, 0:8])
                        if t == 3:
                            tr.then_inc(a_gt, 1)

        with contextlib.ExitStack() as _ps:
            KT_res = _ps.enter_context(nc.sbuf_tensor("KT_res", [128, NDC * S], BF16))
            V_res = _ps.enter_context(nc.sbuf_tensor("V_res", [128, NT * D], BF16))
            qt_buf = _ps.enter_context(nc.sbuf_tensor("qt_buf", [128, 2 * NDC * 256], BF16))
            pt_buf = _ps.enter_context(nc.sbuf_tensor("pt_buf", [128, 4 * 256], BF16))
            xpn_buf = _ps.enter_context(nc.sbuf_tensor("xpn_buf", [128, 2 * 2048], F32))
            xw_buf = _ps.enter_context(nc.sbuf_tensor("xw_buf", [128, 2 * 2048], F32))
            linv = _ps.enter_context(nc.sbuf_tensor("linv", [128, 2], F32))
            pS0 = _ps.enter_context(nc.psum_tensor("pS0", [128, 512], F32))
            pS1 = _ps.enter_context(nc.psum_tensor("pS1", [128, 512], F32))
            pO0 = _ps.enter_context(nc.psum_tensor("pO0", [128, 512], F32))
            pO1 = _ps.enter_context(nc.psum_tensor("pO1", [128, 512], F32))
            pO2 = _ps.enter_context(nc.psum_tensor("pO2", [128, 512], F32))
            pO3 = _ps.enter_context(nc.psum_tensor("pO3", [128, 512], F32))
            pL0 = _ps.enter_context(nc.psum_tensor("pL0", [128, 512], F32))
            pL1 = _ps.enter_context(nc.psum_tensor("pL1", [128, 512], F32))
            blk = _ps.enter_context(nc.Block("phB"))
            pS = [pS0, pS1]
            pO = [pO0, pO1, pO2, pO3]
            pL = [pL0, pL1]

            @blk.sync
            def _(sy):
                sy.wait_ge(st_kt, 128 * NW)
                for dc in range(NDC // 2):
                    sy.dma_start(KT_res[:, dc * S:(dc + 1) * S],
                                 s_KT[dc, :, :]).then_inc(b_ldk, 16)
                for p in range(2):
                    npar = (NW + 1 - p) // 2
                    if npar:
                        sy.wait_ge(st_xpn[p], 64 * npar)
                        sy.wait_ge(st_xnt[p], 128 * npar)
                sy.wait_ge(st_qt, 128 * NW)
                sy.wait_ge(st_xnb, 64 * NW)
                for s in range(min(2, NQS)):
                    for dc in range(NDC):
                        sy.dma_start(qt_buf[:, (s % 2) * 2048 + dc * 256:
                                            (s % 2) * 2048 + (dc + 1) * 256],
                                     s_QT[dc, :, s * 256:(s + 1) * 256]
                                     ).then_inc(b_qt2[s % 2], 16)
                    for q in range(2):
                        i = 2 * s + q
                        sy.dma_start(xpn_buf[:, (s % 2) * 2048 + q * D:
                                             (s % 2) * 2048 + (q + 1) * D],
                                     s_xpn[i * 128:(i + 1) * 128, :]
                                     ).then_inc(b_xp2[s % 2], 16)
                        sy.dma_start(xw_buf[:, (s % 2) * 2048 + q * D:
                                            (s % 2) * 2048 + (q + 1) * D],
                                     x_h[i * 128:(i + 1) * 128, :]
                                     ).then_inc(b_x2[s % 2], 16)
                for s in range(NQS):
                    sy.wait_ge(b_vo2, s + 1)
                    for q in range(2):
                        i = 2 * s + q
                        sy.dma_start(s_pB[i * 128:(i + 1) * 128, :],
                                     xpn_buf[:, (s % 2) * 2048 + q * D:
                                             (s % 2) * 2048 + (q + 1) * D]
                                     ).then_inc(b_st, 16)
                    if s + 2 < NQS:
                        s2 = s + 2
                        sy.wait_ge(b_sht, NKT * (s + 1))
                        for dc in range(NDC):
                            sy.dma_start(qt_buf[:, (s2 % 2) * 2048 + dc * 256:
                                                (s2 % 2) * 2048 + (dc + 1) * 256],
                                         s_QT[dc, :, s2 * 256:(s2 + 1) * 256]
                                         ).then_inc(b_qt2[s2 % 2], 16)
                        sy.wait_ge(b_st, 32 * (s + 1))
                        for q in range(2):
                            i = 2 * s2 + q
                            sy.dma_start(xpn_buf[:, (s2 % 2) * 2048 + q * D:
                                                 (s2 % 2) * 2048 + (q + 1) * D],
                                         s_xpn[i * 128:(i + 1) * 128, :]
                                         ).then_inc(b_xp2[s2 % 2], 16)
                            sy.dma_start(xw_buf[:, (s2 % 2) * 2048 + q * D:
                                                (s2 % 2) * 2048 + (q + 1) * D],
                                         x_h[i * 128:(i + 1) * 128, :]
                                         ).then_inc(b_x2[s2 % 2], 16)

            @blk.gpsimd
            def _(g):
                g.wait_ge(st_kt, 128 * NW)
                for dc in range(NDC // 2, NDC):
                    g.dma_start(KT_res[:, dc * S:(dc + 1) * S],
                                s_KT[dc, :, :]).then_inc(b_ldk2, 16)
                for p in range(2):
                    npar = (NW + 1 - p) // 2
                    if npar:
                        g.wait_ge(st_xpn[p], 64 * npar)
                        g.wait_ge(st_xnt[p], 128 * npar)
                g.wait_ge(st_qt, 128 * NW)
                g.wait_ge(st_xnb, 64 * NW)
                g.wait_ge(st_v, 16 * NW)
                g.dma_start(V_res[:], s_V[:]).then_inc(b_ldv, 16)

            @blk.tensor
            def _(t_):
                t_.wait_ge(b_ldk, 16 * (NDC // 2))
                t_.wait_ge(b_ldk2, 16 * (NDC - NDC // 2))
                for s in range(NQS):
                    par = s % 2
                    t_.wait_ge(b_qt2[s % 2], 16 * NDC * (s // 2 + 1))
                    if s == 0:
                        t_.wait_ge(b_ldv, 16)
                    LAG = 3
                    for k in range(NKT + LAG):
                        if k < NKT:
                            K = NKT * s + k
                            if K >= 2:
                                t_.wait_ge(b_exp, K - 1)
                            for ec in range(NDC):
                                mm = t_.matmul(pS[k % 2][:, 0:256],
                                               KT_res[:, ec * S + k * 128:
                                                      ec * S + (k + 1) * 128],
                                               qt_buf[:, par * 2048 + ec * 256:
                                                      par * 2048 + (ec + 1) * 256],
                                               start=(ec == 0), stop=(ec == NDC - 1))
                            mm.then_inc(b_sht, 1)
                        j = k - LAG
                        if j >= 0:
                            KJ = NKT * s + j
                            if j == 0 and s >= 1:
                                t_.wait_ge(b_vo, s)
                            t_.wait_ge(b_exp, KJ + 1)
                            for q in range(2):
                                for h in range(2):
                                    t_.matmul(pO[2 * q + h][:, 0:512],
                                              pt_buf[:, (KJ % 4) * 256 + q * 128:
                                                     (KJ % 4) * 256 + (q + 1) * 128],
                                              V_res[:, j * D + h * 512:
                                                    j * D + (h + 1) * 512],
                                              start=(j == 0), stop=(j == NKT - 1))
                                mm = t_.matmul(pL[q][:, 0:1],
                                               pt_buf[:, (KJ % 4) * 256 + q * 128:
                                                      (KJ % 4) * 256 + (q + 1) * 128],
                                               ones_s[:, 0:1],
                                               start=(j == 0), stop=(j == NKT - 1))
                            mm.then_inc(b_pv, 1)

            @blk.scalar
            def _(s_):
                for s in range(NQS):
                    for k in range(NKT):
                        K = NKT * s + k
                        s_.wait_ge(b_sht, K + 1)
                        if K >= 4:
                            s_.wait_ge(b_pv, K - 3)
                        s_.activation(pt_buf[:, (K % 4) * 256:(K % 4) * 256 + 256],
                                      pS[k % 2][:, 0:256], Act.Exp,
                                      bias=cc(C_NEG8),
                                      scale=cc(C_1_32)).then_inc(b_exp, 1)

            @blk.vector
            def _(v):
                for s in range(NQS):
                    par = s % 2
                    v.wait_ge(b_pv, NKT * (s + 1))
                    v.wait_ge(b_xp2[s % 2], 32 * (s // 2 + 1))
                    v.drain()
                    v.reciprocal(linv[:, 0:1], pL[0][:, 0:1])
                    v.reciprocal(linv[:, 1:2], pL[1][:, 0:1])
                    v.drain()
                    for q in range(2):
                        for h in range(2):
                            dst = xpn_buf[:, par * 2048 + q * D + h * 512:
                                          par * 2048 + q * D + (h + 1) * 512]
                            stt = v.scalar_tensor_tensor(
                                dst, pO[2 * q + h][:, 0:512], linv[:, q:q + 1],
                                dst, op0=Alu.mult, op1=Alu.add)
                    stt.then_inc(b_vo, 1)
                    v.drain()
                    v.wait_ge(b_x2[s % 2], 32 * (s // 2 + 1))
                    for q in range(2):
                        dst = xpn_buf[:, par * 2048 + q * D:par * 2048 + (q + 1) * D]
                        tt = v.tensor_tensor(dst, dst,
                                             xw_buf[:, par * 2048 + q * D:
                                                    par * 2048 + (q + 1) * D],
                                             op=Alu.add)
                    tt.then_inc(b_vo2, 1)

        with contextlib.ExitStack() as _ps:
            slot_pi = _ps.enter_context(nc.sbuf_tensor("slot_pi", [128, NT], F32))
            slot_u = _ps.enter_context(nc.sbuf_tensor("slot_u", [128, NT], mybir.dt.uint32))
            perm_pi = _ps.enter_context(nc.sbuf_tensor("perm_pi", [128, NCT], F32))
            perm_u = _ps.enter_context(nc.sbuf_tensor("perm_u", [128, NCT], mybir.dt.uint32))
            wE = _ps.enter_context(nc.sbuf_tensor("wE", [128, 2 * NDC * D], BF16))
            xg = _ps.enter_context(nc.sbuf_tensor("xg", [128, 4 * D], BF16))
            xgT = _ps.enter_context(nc.sbuf_tensor("xgT", [128, 2 * D], BF16))
            y_sb = _ps.enter_context(nc.sbuf_tensor("y_sb", [128, 2 * D], F32))
            _setup_es = contextlib.ExitStack()
            ch8 = _setup_es.enter_context(nc.sbuf_tensor("ch8", [8, S], F32))
            mask8 = _setup_es.enter_context(nc.sbuf_tensor("mask8", [8, S], F32))
            incl8 = _setup_es.enter_context(nc.sbuf_tensor("incl8", [8, S], F32))
            rank_sb = _setup_es.enter_context(nc.sbuf_tensor("rank_sb", [1, S], F32))
            slot_row = _setup_es.enter_context(nc.sbuf_tensor("slot_row", [1, S], F32))
            pfb = _setup_es.enter_context(nc.sbuf_tensor("pfb", [CAPTOT // 128, 128], F32))
            tr1_sb = _setup_es.enter_context(nc.sbuf_tensor("tr1_sb", [NT, 128], F32))
            tr2_sb = _setup_es.enter_context(nc.sbuf_tensor("tr2_sb", [NT, 128], F32))
            pp_sb = _setup_es.enter_context(nc.sbuf_tensor("pp_sb", [CAPTOT // 128, 128], F32))
            _setup_es.close()
            pg = _ps.enter_context(nc.sbuf_tensor("pg", [128, 8 * D], F32))
            cmask = _ps.enter_context(nc.sbuf_tensor("cmask", [128, 1], F32))
            pCt0 = _ps.enter_context(nc.psum_tensor("pCt0", [128, 1024], BF16))
            pCt1 = _ps.enter_context(nc.psum_tensor("pCt1", [128, 1024], BF16))
            pC4 = _ps.enter_context(nc.psum_tensor("pC4", [128, 512], F32))
            pC5 = _ps.enter_context(nc.psum_tensor("pC5", [128, 512], F32))
            pC6 = _ps.enter_context(nc.psum_tensor("pC6", [128, 512], F32))
            pC7 = _ps.enter_context(nc.psum_tensor("pC7", [128, 512], F32))
            blk = _ps.enter_context(nc.Block("phC"))
            pCt = [pCt0, pCt1]
            pCy = [pC4, pC5, pC6, pC7]
            pR = [pC4, pC5]

            NRK = S // 512

            @blk.sync
            def _(sy):
                sy.wait_ge(b_st, 32 * NQS)
                sy.wait_ge(c_e1, 1)
                sy.dma_start(s_m1[:, :], tr1_sb[:]).then_inc(c_s1, 16)
                sy.wait_ge(c_s1, 16)
                m1_t = bass.AP(s_m1.tensor, 0, [[S, 1], [1, S]])
                for r in range(8):
                    sy.dma_start(ch8[r:r + 1, :], m1_t).then_inc(c_s2, 16)
                sy.wait_ge(c_v2, 1)
                sy.dma_start(s_m2[0:1, :], slot_row[0:1, :]).then_inc(c_s3, 16)
                sy.wait_ge(c_s3, 16)
                m2v = bass.AP(s_m2.tensor, 0, [[128, NT], [1, 128]])
                sy.dma_start(tr2_sb[:], m2v).then_inc(c_s4, 16)
                sy.wait_ge(c_gs, 16 * NT)
                sy.dma_start(pp_sb[:], s_perm).then_inc(c_s5, 16)

            @blk.gpsimd
            def _(g_):
                g_.wait_ge(b_st, 32 * NQS)
                g_.dma_start(pfb[:], dumpf_h).then_inc(c_gp, 16)
                g_.wait_ge(c_gp, 16)
                g_.dma_start(s_perm, pfb[:]).then_inc(c_gp, 16)
                g_.wait_ge(c_gp, 32)
                g_.wait_ge(c_v3, 1)
                pflat = bass.AP(s_perm.tensor, 0, [[CAPTOT, 1], [1, CAPTOT]])
                for i in range(NT):
                    g_.indirect_dma_start(
                        pflat, bass.IndirectOffsetOnAxis(slot_u[:, i:i + 1], 1),
                        iotat_s[:, i:i + 1], None).then_inc(c_gs, 16)
                def exp_load(e):
                    if e >= 2:
                        g_.wait_ge(c_my, CAPT * (e - 1))
                    for dc in range(NDC):
                        g_.dma_start(wE[:, (e % 2) * NDC * D + dc * D:
                                        (e % 2) * NDC * D + (dc + 1) * D],
                                     ewT_h[e, dc * 128:(dc + 1) * 128, :]
                                     ).then_inc(c_wl[e % 2], 16)

                exp_load(0)
                exp_load(1)
                g_.wait_ge(st_xnb, 64 * NW)
                g_.wait_ge(c_v4, 1)
                g_.wait_ge(c_z, 16)
                _ins = {CAPT * (e - 1) + max(1, CAPT // 2): e
                        for e in range(2, E)}
                bnd_r = g_.alloc_register(name="bnd")
                g_.reg_mov(bnd_r, S - 1)

                def out_scat(ct):
                    g_.wait_ge(c_oa, ct + 1)
                    g_.indirect_dma_start(
                        out_h, bass.IndirectOffsetOnAxis(perm_u[:, ct:ct + 1], 0),
                        pg[:, (ct % 8) * D:(ct % 8 + 1) * D], None,
                        bounds_check=bnd_r, oob_is_err=False
                    ).then_inc(c_os[ct % 8], 16)

                for ct in range(NCT):
                    if ct in _ins:
                        exp_load(_ins[ct])
                    if ct >= 4:
                        g_.wait_ge(c_tp2, ct - 3)
                    g_.indirect_dma_start(
                        xg[:, (ct % 4) * D:(ct % 4 + 1) * D], None,
                        s_xnb, bass.IndirectOffsetOnAxis(perm_u[:, ct:ct + 1], 0)
                    ).then_inc(c_g2[ct % 4], 16)
                    if ct >= 8:
                        g_.wait_ge(c_os[ct % 8], 16 * ((ct - 8) // 8 + 1))
                    g_.indirect_dma_start(
                        pg[:, (ct % 8) * D:(ct % 8 + 1) * D], None,
                        s_pB, bass.IndirectOffsetOnAxis(perm_u[:, ct:ct + 1], 0)
                    ).then_inc(c_pg[ct % 8], 16)
                    if ct >= 6:
                        out_scat(ct - 6)
                for j in range(NCT - 6, NCT):
                    out_scat(j)

            @blk.vector
            def _(v):
                v.wait_ge(c_t1, 1)
                v.tensor_copy(tr1_sb[:], pR[0][0:NT, 0:128]).then_inc(c_e1, 1)
                v.wait_ge(c_s2, 128)
                v.tensor_scalar(mask8[:], ch8[:], iotac_s[0:8, 0:1], None,
                                op0=Alu.is_equal)
                v.drain()
                v.tensor_tensor_scan(incl8[:], mask8[:], mask8[:], 0.0,
                                     op0=Alu.add, op1=Alu.bypass)
                v.drain()
                v.tensor_tensor(mask8[:], mask8[:], incl8[:],
                                op=Alu.mult).then_inc(c_v1, 1)
                for j in range(NRK):
                    v.wait_ge(c_mr, j + 1)
                    cp = v.tensor_copy(rank_sb[0:1, j * 512:(j + 1) * 512],
                                       pR[j % 2][0:1, 0:512])
                    cp.then_inc(c_rv, 1)
                v.drain()
                v.tensor_scalar(slot_row[0:1, :], ch8[0:1, :], float(CAP), -1.0,
                                op0=Alu.mult, op1=Alu.add)
                v.drain()
                v.tensor_tensor(slot_row[0:1, :], slot_row[0:1, :], rank_sb[0:1, :],
                                op=Alu.add)
                v.drain()
                v.tensor_scalar(slot_row[0:1, :], slot_row[0:1, :],
                                float(CAPTOT - 1), None,
                                op0=Alu.min).then_inc(c_v2, 1)
                v.wait_ge(c_t2, 1)
                v.tensor_copy(slot_pi[:], pR[0][:, 0:NT])
                v.drain()
                v.tensor_copy(slot_u[:], slot_pi[:]).then_inc(c_v3, 1)
                v.wait_ge(c_t3, 1)
                v.tensor_copy(perm_pi[:], pR[1][:, 0:NCT])
                v.drain()
                v.tensor_copy(perm_u[:], perm_pi[:]).then_inc(c_v4, 1)
                for ct in range(NCT):
                    v.wait_ge(c_ye, ct + 1)
                    v.wait_ge(c_pg[ct % 8], 16 * (ct // 8 + 1))
                    v.drain()
                    dst = pg[:, (ct % 8) * D:(ct % 8 + 1) * D]
                    v.tensor_tensor(dst, dst,
                                    y_sb[:, (ct % 2) * D:(ct % 2 + 1) * D],
                                    op=Alu.add).then_inc(c_oa, 1)

            @blk.scalar
            def _(s_):
                def s_xgt(ct):
                    s_.wait_ge(c_tp2, ct + 1)
                    if ct >= 2:
                        s_.wait_ge(c_my, ct - 1)
                    aa = s_.activation(
                        xgT[:, (ct % 2) * D:(ct % 2 + 1) * D],
                        pCt[ct % 2][:, :], Act.Identity,
                        bias=cc(C_ZERO), scale=cc(C_ONE))
                    aa.then_inc(c_xe, 1)

                s_xgt(0)
                for ct in range(NCT):
                    if ct + 1 < NCT:
                        s_xgt(ct + 1)
                    s_.wait_ge(c_my, ct + 1)
                    if ct >= 2:
                        s_.wait_ge(c_oa, ct - 1)
                    for hh in range(2):
                        aa = s_.activation(
                            y_sb[:, (ct % 2) * D + hh * 512:(ct % 2) * D + (hh + 1) * 512],
                            pCy[2 * (ct % 2) + hh][:, 0:512], Act.Identity,
                            bias=cc(C_ZERO), scale=cc(C_ONE))
                    aa.then_inc(c_ye, 1)
                for p in range(8):
                    n = (NCT - p + 7) // 8
                    if n:
                        s_.wait_ge(c_os[p], 16 * n)

            @blk.tensor
            def _(t_):
                t_.transpose(pR[0][0:NT, 0:128], idx_res[:], ident[:]).then_inc(c_t1, 1)
                t_.wait_ge(c_v1, 1)
                for j in range(NRK):
                    if j == 0:
                        t_.wait_ge(c_e1, 1)
                    if j >= 2:
                        t_.wait_ge(c_rv, j - 1)
                    t_.matmul(pR[j % 2][0:1, 0:512], consts[0:8, C_ONE:C_ONE + 1],
                              mask8[:, j * 512:(j + 1) * 512],
                              start=True, stop=True).then_inc(c_mr, 1)
                t_.wait_ge(c_rv, NRK)
                t_.wait_ge(c_s4, 16)
                t_.transpose(pR[0][:, 0:NT], tr2_sb[:],
                             ident[0:NT, 0:NT]).then_inc(c_t2, 1)
                t_.wait_ge(c_v3, 1)
                for k in range(3 * NT):
                    t_.matmul(pR[0][0:1, 0:128], ident[:, 0:1],
                              ident[:, 0:128], start=True, stop=True)
                t_.wait_ge(c_s5, 16)
                t_.transpose(pR[1][:, 0:NCT], pp_sb[:],
                             ident[0:CAPTOT // 128, 0:CAPTOT // 128]).then_inc(c_t3, 1)
                t_.wait_ge(c_v3, 1)
                t_.wait_ge(c_v4, 1)

                def tp2(ct):
                    t_.wait_ge(c_g2[ct % 4], 16 * (ct // 4 + 1))
                    if ct >= 2:
                        t_.wait_ge(c_xe, ct - 1)
                    for dc in range(8):
                        tr = t_.transpose(
                            pCt[ct % 2][:, dc * 128:(dc + 1) * 128],
                            xg[:, (ct % 4) * D + dc * 128:(ct % 4) * D + (dc + 1) * 128],
                            identb[:])
                        if dc == 7:
                            tr.then_inc(c_tp2, 1)

                tp2(0)
                tp2(1)
                for ct in range(NCT):
                    e = ct // CAPT
                    t_.wait_ge(c_wl[e % 2], 16 * NDC * (e // 2 + 1))
                    t_.wait_ge(c_xe, ct + 1)
                    if ct >= 2:
                        t_.wait_ge(c_ye, ct - 1)
                    for dc in range(NDC):
                        for hh in range(2):
                            mm = t_.matmul(
                                pCy[2 * (ct % 2) + hh][:, 0:512],
                                xgT[:, (ct % 2) * D + dc * 128:(ct % 2) * D + (dc + 1) * 128],
                                wE[:, (e % 2) * NDC * D + dc * D + hh * 512:
                                   (e % 2) * NDC * D + dc * D + (hh + 1) * 512],
                                start=(dc == 0), stop=(dc == NDC - 1))
                    mm.then_inc(c_my, 1)
                    if ct + 2 < NCT:
                        tp2(ct + 2)

    return nc


def _host_inputs(inputs, S=4096):
    f32 = np.float32
    Wq = np.asarray(inputs["L0_attn_Wq"], f32)
    Wk = np.asarray(inputs["L0_attn_Wk"], f32)
    Wv = np.asarray(inputs["L0_attn_Wv"], f32)
    bq = np.asarray(inputs["L0_attn_bq"], f32)
    bk = np.asarray(inputs["L0_attn_bk"], f32)
    bv = np.asarray(inputs["L0_attn_bv"], f32)
    gw = np.asarray(inputs["L0_ffn_gate_w"], f32)
    ew = np.asarray(inputs["L0_ffn_expert_w"], f32)
    shared = {
        "wqT": np.ascontiguousarray(Wq.T),
        "wkT": np.ascontiguousarray(Wk.T),
        "wvT": np.ascontiguousarray(Wv.T),
        "ewT": np.ascontiguousarray(ew.transpose(0, 2, 1)),
        "gwT": np.ascontiguousarray(gw.reshape(8, 8, 128).transpose(2, 1, 0)
                                    .reshape(128, 64)),
        "bqc": np.ascontiguousarray(bq.reshape(8, 128).T),
        "bkc": np.ascontiguousarray(bk.reshape(8, 128).T),
        "bvb": np.ascontiguousarray(np.broadcast_to(bv, (128, D))),
        "cst": np.ascontiguousarray(np.broadcast_to(
            np.array(CONST_VALS + [0.0], f32), (128, 8))),
        "idm": np.eye(128, dtype=f32),
        "idmb": np.eye(128, dtype=ml_dtypes.bfloat16),
        "onesb": np.ones((128, 1), dtype=ml_dtypes.bfloat16),
        "iotac": np.arange(128, dtype=f32).reshape(128, 1),
        "iotat": np.ascontiguousarray(
            (np.arange(S // 128, dtype=f32)[None, :] * 128
             + np.arange(128, dtype=f32)[:, None])),
        "dumpf": np.full((2 * S // 128, 128), float(S), dtype=f32),
        "zrow": np.zeros((1, D), dtype=ml_dtypes.bfloat16),
        "zrowf": np.zeros((1, D), dtype=np.float32),
    }
    return shared


_NC_CACHE = {}


def kernel(L0_attn_x, L0_attn_Wq, L0_attn_bq, L0_attn_Wk, L0_attn_bk,
           L0_attn_Wv, L0_attn_bv, L0_ffn_gate_w, L0_ffn_expert_w):
    S = 4096
    inputs = dict(L0_attn_x=L0_attn_x, L0_attn_Wq=L0_attn_Wq, L0_attn_bq=L0_attn_bq,
                  L0_attn_Wk=L0_attn_Wk, L0_attn_bk=L0_attn_bk, L0_attn_Wv=L0_attn_Wv,
                  L0_attn_bv=L0_attn_bv, L0_ffn_gate_w=L0_ffn_gate_w,
                  L0_ffn_expert_w=L0_ffn_expert_w)
    shared = _host_inputs(inputs, S)
    x = np.asarray(L0_attn_x, np.float32)
    in_maps = []
    for b in range(8):
        m = dict(shared)
        m["x"] = np.ascontiguousarray(x[b])
        in_maps.append(m)
    if S not in _NC_CACHE:
        _NC_CACHE[S] = build_nc(S)
    nc = _NC_CACHE[S]
    res = run_bass_kernel_spmd(nc, in_maps, core_ids=list(range(8)))
    out = np.stack([np.asarray(r["out"], np.float32)[:S] for r in res.results])
    return out
